# version 08
# baseline (speedup 1.0000x reference)
from collections import defaultdict

import numpy as np
import ml_dtypes

import concourse.bass as bass
import concourse.mybir as mybir
from concourse.bass_utils import run_bass_kernel_spmd
from concourse.aot_env import allow_ici_all_to_all

F32 = mybir.dt.float32
BF16 = mybir.dt.bfloat16
AF = mybir.ActivationFunctionType
ALU = mybir.AluOpType
AX = mybir.AxisListType.X

S = 8192
D = 2048
NCORE = 8
TOK = S // NCORE
NT = 16
EPS = 1e-5
EXP_SCALE = 0.125
EXP_BIAS = -12.0
MASK_NEG = -30000.0


def build(taps=(), phases=5, nt=NT, p5_parts='rdwy', skip=()):
    nc = bass.Bass("TRN2", target_bir_lowering=False)

    d_x = nc.dram_tensor("x", [S, D], F32, kind="ExternalInput")
    d_xo = nc.dram_tensor("x_own", [TOK, D], F32, kind="ExternalInput")
    d_wqkv = nc.dram_tensor("wqkv", [D, 768], F32, kind="ExternalInput")
    d_bqkv = nc.dram_tensor("bqkv", [128, 6], F32, kind="ExternalInput")
    d_wout = nc.dram_tensor("wout", [4, D, 512], F32, kind="ExternalInput")
    d_wgate = nc.dram_tensor("wgate", [D, 8], F32, kind="ExternalInput")
    d_bout = nc.dram_tensor("bout", [1, D], F32, kind="ExternalInput")
    d_wgu = nc.dram_tensor("wgu", [8, 8, 16, 128, 256], F32, kind="ExternalInput")
    d_wdn = nc.dram_tensor("wdn", [8, 4, 8, 128, 512], F32, kind="ExternalInput")
    d_wsg = nc.dram_tensor("wsg", [4, 8, 16, 128, 256], F32, kind="ExternalInput")
    d_wsd = nc.dram_tensor("wsd", [4, 4, 8, 128, 512], F32, kind="ExternalInput")
    d_cs1 = nc.dram_tensor("cs1", [128, S], BF16, kind="ExternalInput")
    d_cs2 = nc.dram_tensor("cs2", [128, S], BF16, kind="ExternalInput")
    d_masks = nc.dram_tensor("masks", [4, 128, 512], BF16, kind="ExternalInput")
    d_ident = nc.dram_tensor("identb", [128, 128], BF16, kind="ExternalInput")
    d_ident32 = nc.dram_tensor("ident32", [128, 128], F32, kind="ExternalInput")
    d_swap = nc.dram_tensor("swapm", [128, 128], BF16, kind="ExternalInput")
    d_sel2 = nc.dram_tensor("sel2", [2, 128], BF16, kind="ExternalInput")
    d_ones1 = nc.dram_tensor("ones1", [1, 128], F32, kind="ExternalInput")

    d_y = nc.dram_tensor("y", [TOK, D], F32, kind="ExternalOutput")

    d_a2a_in = nc.dram_tensor("a2a_in", [D, TOK], BF16, kind="Internal")
    d_a2a_out = nc.dram_tensor("a2a_out", [D, TOK], BF16, kind="Internal")
    d_rl_in = nc.dram_tensor("rl_in", [64, 512], BF16, kind="Internal")
    d_rl_out = nc.dram_tensor("rl_out", [64, 512], BF16, kind="Internal")

    tap_t = {}
    if "qkv" in taps:
        tap_t["QT0"] = nc.dram_tensor("tQT0", [128, S], F32, kind="ExternalOutput")
        tap_t["KT1"] = nc.dram_tensor("tKT1", [128, S], F32, kind="ExternalOutput")
        tap_t["V0"] = nc.dram_tensor("tV0", [128, 64 * 65], F32, kind="ExternalOutput")
        tap_t["QT1"] = nc.dram_tensor("tQT1", [128, S], F32, kind="ExternalOutput")
        tap_t["KT0"] = nc.dram_tensor("tKT0", [128, S], F32, kind="ExternalOutput")
        tap_t["V2"] = nc.dram_tensor("tV2", [128, 64 * 65], F32, kind="ExternalOutput")
    if "attn" in taps:
        tap_t["A0"] = nc.dram_tensor("tA0", [128, S], F32, kind="ExternalOutput")
        tap_t["A1"] = nc.dram_tensor("tA1", [128, S], F32, kind="ExternalOutput")
        tap_t["L"] = nc.dram_tensor("tL", [64, 512], F32, kind="ExternalOutput")
    if "rb" in taps:
        tap_t["RB"] = nc.dram_tensor("tRB", [128, 16 * 1024], F32, kind="ExternalOutput")
        tap_t["CT"] = nc.dram_tensor("tCT", [8, 1024], F32, kind="ExternalOutput")
    if "rdbg" in taps:
        tap_t["LG"] = nc.dram_tensor("tLG", [8, 1024], F32, kind="ExternalOutput")
        tap_t["ES"] = nc.dram_tensor("tES", [128, 64], F32, kind="ExternalOutput")
        tap_t["CA"] = nc.dram_tensor("tCA", [128, 64], F32, kind="ExternalOutput")
        tap_t["CT2"] = nc.dram_tensor("tCT2", [8, 1024], F32, kind="ExternalOutput")
    if "p1dbg" in taps:
        tap_t["XS"] = nc.dram_tensor("tXS", [128, 2048], F32, kind="ExternalOutput")
        tap_t["XN"] = nc.dram_tensor("tXN", [128, 2048], F32, kind="ExternalOutput")
        tap_t["ST"] = nc.dram_tensor("tST", [128, 16], F32, kind="ExternalOutput")
        tap_t["XT"] = nc.dram_tensor("tXT", [128, 16 * 512], F32, kind="ExternalOutput")

    A = nc.alloc_sbuf_tensor
    AT = nc.alloc_sbuf_tensor_at
    ident = A("identS", [128, 128], BF16)
    ident32 = A("ident32S", [128, 128], F32)
    swapm = A("swapS", [128, 128], BF16)
    sel2 = A("sel2S", [2, 128], BF16)
    ones1 = A("ones1S", [1, 128], F32)
    bias_qkv = A("biasqkvS", [128, 6], F32)
    wgate_sb = A("wgateS", [128, 16, 8], BF16)
    bout_b = A("boutbS", [128, 2048], BF16)
    e_sb = A("esbS", [128, 64], F32)
    comb_all = A("comballS", [128, 64], F32)
    combine_T = A("combTS", [8, 1024], F32)
    c_bcast = A("cbcastS", [128, 1024], BF16)
    stats = A("statsS", [128, 16], F32)
    ebias = A("ebiasS", [128, 1], F32)
    sums_a = A("sumsS", [128, 64], F32)
    ssqs_a = A("ssqsS", [128, 64], F32)
    means_a = A("meansS", [128, 64], F32)
    rstds_a = A("rstdsS", [128, 64], F32)

    stage_l = [A(f"stagelS{i}", [1, 512], F32) for i in range(2)]

    R_S = A("R_S", [128, 10240], mybir.dt.uint8)
    adr_s = nc.lookup_mloc(R_S).addr
    cs1_t = [AT(f"cs1_{i}", [128, 512], BF16, offset=adr_s + 1024 * i) for i in range(2)]
    cs2_t = [AT(f"cs2_{i}", [128, 512], BF16, offset=adr_s + 2048 + 1024 * i) for i in range(2)]
    swp_t = [AT(f"swp_{i}", [128, 512], BF16, offset=adr_s + 4096 + 1024 * i) for i in range(2)]
    vtmp_t = [AT(f"vtmp_{i}", [128, 512], BF16, offset=adr_s + 6144 + 1024 * i) for i in range(2)]
    rope1 = AT("rope1", [128, 512], BF16, offset=adr_s + 8192)
    rope2 = AT("rope2", [128, 512], BF16, offset=adr_s + 9216)
    masks_sb = AT("masksS", [128, 4, 512], BF16, offset=adr_s + 0)
    pstrip = [AT(f"pstrip_{i}", [128, 1024], BF16, offset=adr_s + 4096 + 2048 * i)
              for i in range(3)]
    sg_t = [AT(f"sgt{i}", [128, 1024], BF16, offset=adr_s + 2048 * i) for i in range(2)]
    comb_stage = AT("combstS", [1, 1024], F32, offset=adr_s + 4096)

    R_A = A("R_A", [128, 98816], mybir.dt.uint8)
    adr_a = nc.lookup_mloc(R_A).addr
    QT = [AT(f"QT{i}", [128, S], BF16, offset=adr_a + 16384 * i) for i in range(2)]
    KT = [AT(f"KT{i}", [128, S], BF16, offset=adr_a + 32768 + 16384 * i) for i in range(2)]
    V65 = [AT(f"V65_{h}", [128, 64 * 65], BF16, offset=adr_a + 65536 + 8320 * h)
           for h in range(4)]
    Y_sb = AT("Ysb", [128, 8, 2048], F32, offset=adr_a + 0)
    act_sb = AT("actsb", [128, 8, 1024], BF16, offset=adr_a + 65536)
    rl_sb = AT("rlsb", [2, 16, 1024], BF16, offset=adr_a + 65536)

    R_B = A("R_B", [128, 40960], mybir.dt.uint8)
    adr_b = nc.lookup_mloc(R_B).addr
    x_stage = [AT(f"xst{i}", [128, 2048], F32, offset=adr_b + 8192 * i) for i in range(2)]
    xn_t = [AT(f"xnt{i}", [128, 2048], BF16, offset=adr_b + 16384 + 4096 * i)
            for i in range(2)]
    xnT_cur = AT("xnTcur", [128, 16, 512], BF16, offset=adr_b + 24576)
    A_t = [AT(f"At{i}", [128, 8, 1024], BF16, offset=adr_b + 16384 * i) for i in range(2)]
    xn_ownT = AT("xnownT", [128, 16, 1024], BF16, offset=adr_b + 0)
    AT_rb = AT("ATrb", [128, 16, 1024], BF16, offset=adr_b + 0)

    R_C = A("R_C", [128, 32768], mybir.dt.uint8)
    adr_c = nc.lookup_mloc(R_C).addr
    wqkv_sb = AT("wqkvS", [128, 16, 768], BF16, offset=adr_c)
    l_buf = AT("lbufS", [64, 512], F32, offset=adr_c + 0)
    rl_buf = AT("rlbufS", [64, 512], BF16, offset=adr_c + 2048)
    logits_sb = AT("logitsS", [8, 1024], F32, offset=adr_c + 3072)
    xo_stage = [AT(f"xost{i}", [128, 2048], F32, offset=adr_c + 8192 * i) for i in range(2)]
    xn_to = [AT(f"xnto{i}", [128, 2048], BF16, offset=adr_c + 16384 + 4096 * i)
             for i in range(2)]
    bout_sb = AT("boutS", [1, 2048], F32, offset=adr_c + 24576)
    x_stage2 = [AT("xst2_0", [128, 2048], F32, offset=adr_c + 24576),
                A("xst2_1", [128, 2048], F32)]
    sgw = [AT(f"sgw{i}", [128, 16, 256], BF16, offset=adr_c + 8192 * i) for i in range(2)]
    sdw = [AT(f"sdw{i}", [128, 8, 512], BF16, offset=adr_c + 16384 + 8192 * i)
           for i in range(2)]
    wob = [AT(f"wob{i}", [128, 16, 512], BF16, offset=adr_c + 16384 * i) for i in range(2)]

    PA = nc.alloc_psum_tensor("PA", [128, 1024], F32)
    PB = nc.alloc_psum_tensor("PB", [128, 1024], F32)
    PC = nc.alloc_psum_tensor("PC", [128, 1024], F32)
    b6 = nc.psum_base
    PD32 = nc.alloc_psum_tensor("PD32", [128, 1024], F32)
    nc.psum_base = b6
    PDT = nc.alloc_psum_tensor("PDT", [128, 2048], BF16)
    qk_banks = [PA[:, 0:512], PA[:, 512:1024], PB[:, 0:512], PB[:, 512:1024]]

    sems = {}
    for n in ("dx0", "dx1", "dcs0", "dcs1", "dwq", "dmisc", "dmsk", "dcb",
              "dxo0", "dxo1", "dgu0", "dgu1", "ddn0", "ddn1",
              "dst", "drb", "dyo", "dtap",
              "c_sq", "c_vr", "c_rs", "c_ln", "c_tp", "c_te", "c_qk", "c_qe",
              "c_sw", "c_se", "c_ro", "c_vt", "c_ve",
              "c_ss", "c_mk", "c_ex", "c_pv", "c_oe", "c_rl", "c_cc",
              "c_rg", "c_rte", "c_rex", "c_rtr", "c_rv", "c_rt2", "c_rce",
              "c_bc", "c_be", "c_gu", "c_si", "c_am", "c_dn", "c_ya",
              "c_dv", "c_wo", "c_vq", "c_lx", "dlx", "bar"):
        sems[n] = nc.alloc_semaphore(n)

    led = defaultdict(int)
    plan = {k: [] for k in "TVAYG"}

    def load_act_set(e, idx):
        return e.add_instruction(mybir.InstLoadActFuncSet(
            name=nc.get_next_instruction_name(), act_func_set_id=idx,
            ins=[], outs=[]))

    def run(eng, fn):
        plan[eng].append(fn)

    def wait(eng, name, val):
        if val > 0:
            plan[eng].append(lambda e, s=sems[name], v=val: e.wait_ge(s, v))

    def step(eng, fn, inc=None, incn=1):
        if inc is None:
            plan[eng].append(fn)
        else:
            led[inc] += incn
            plan[eng].append(
                lambda e, f=fn, ss=sems[inc], n=incn: f(e).then_inc(ss, n))

    def vstep(fn):
        step("V", fn, "c_vq")
        wait("V", "c_vq", led["c_vq"])

    def barrier():
        tgt = led["bar"] + 5
        led["bar"] += 5
        for k in "TVAYG":
            plan[k].append(lambda e, ss=sems["bar"]: e.nop().then_inc(ss, 1))
        for k in "TVAYG":
            plan[k].append(lambda e, s=sems["bar"], v=tgt: e.wait_ge(s, v))

    def p0():
        step("Y", lambda e: e.dma_start(out=ident[:, :], in_=d_ident.ap()),
             "dmisc", 16)
        step("Y", lambda e: e.dma_start(out=ident32[:, :], in_=d_ident32.ap()),
             "dmisc", 16)
        step("Y", lambda e: e.dma_start(out=swapm[:, :], in_=d_swap.ap()),
             "dmisc", 16)
        step("Y", lambda e: e.dma_start(out=sel2[:, :], in_=d_sel2.ap()),
             "dmisc", 16)
        step("Y", lambda e: e.dma_start(out=ones1[:, :], in_=d_ones1.ap()),
             "dmisc", 16)
        step("Y", lambda e: e.dma_start(out=bias_qkv[:, :], in_=d_bqkv.ap()),
             "dmisc", 16)
        step("G", lambda e: e.dma_start(
            out=wgate_sb[:, :, :],
            in_=d_wgate.ap().rearrange("(f p) c -> p f c", p=128)), "dmisc", 16)
        step("G", lambda e: e.dma_start(
            out=wqkv_sb[:, :, :],
            in_=d_wqkv.ap().rearrange("(f p) c -> p f c", p=128)), "dwq", 16)
        step("Y", lambda e: e.dma_start(out=cs1_t[0][:, :], in_=d_cs1.ap()[:, 0:512]),
             "dcs0", 16)
        step("Y", lambda e: e.dma_start(out=cs2_t[0][:, :], in_=d_cs2.ap()[:, 0:512]),
             "dcs0", 16)
        for h in range(4):
            step("V", lambda e, h=h: e.memset(V65[h][:, :], 1.0), "c_ve")
        step("V", lambda e: e.memset(ebias[:, :], EXP_BIAS), "c_rl")
        run("A", lambda e: load_act_set(e, 6))

    def p1():
        wait("T", "dmisc", led["dmisc"])
        wait("A", "dmisc", led["dmisc"])
        pdt_evac = [0, 0]
        qkb_uses = [0] * 4
        swp_uses = [0, 0]
        swp_rope = [0, 0]
        cs_ro = [0, 0]
        for T in range(nt):
            stg = [x_stage[0], x_stage[1], x_stage2[0], x_stage2[1]]
            for st in range(4):
                g = 4 * T + st
                par = g % 4
                if g >= 4:
                    wait("Y", "c_sq", g - 3)
                    wait("Y", "c_ln", g - 3)
                step("Y", lambda e, g=g, par=par: e.dma_start(
                    out=stg[par][:, :], in_=d_x.ap()[128 * g:128 * (g + 1), :]),
                    "dx%d" % (par % 2), 16)
                wait("A", "dx%d" % (par % 2), led["dx%d" % (par % 2)])
                step("A", lambda e, g=g, par=par: e.activation(
                    bout_b[:, :], stg[par][:, :], AF.Square,
                    accum_out=ssqs_a[:, g:g + 1]), "c_sq")
                wait("V", "dx%d" % (par % 2), led["dx%d" % (par % 2)])
                step("V", lambda e, g=g, par=par: e.reduce_sum(
                    sums_a[:, g:g + 1], stg[par][:, :], axis=AX), "c_vq")
            wait("V", "c_vq", led["c_vq"])
            wait("V", "c_sq", 4 * (T + 1))
            sl = slice(4 * T, 4 * T + 4)
            vstep(lambda e, sl=sl: e.tensor_scalar(
                means_a[:, sl], sums_a[:, sl], 1.0 / D, None, op0=ALU.mult))
            vstep(lambda e, sl=sl: e.tensor_scalar(
                rstds_a[:, sl], ssqs_a[:, sl], 1.0 / D, None, op0=ALU.mult))
            vstep(lambda e, sl=sl: e.tensor_mul(
                stats[:, 0:4], means_a[:, sl], means_a[:, sl]))
            vstep(lambda e, sl=sl: e.tensor_sub(
                rstds_a[:, sl], rstds_a[:, sl], stats[:, 0:4]))
            step("V", lambda e, sl=sl: e.tensor_scalar(
                rstds_a[:, sl], rstds_a[:, sl], EPS, None, op0=ALU.add), "c_vr")
            wait("A", "c_vr", led["c_vr"])

            def act_rstd(e, sl=sl):
                e.activation(stats[:, 4:8], rstds_a[:, sl], AF.Ln)
                return e.activation(rstds_a[:, sl], stats[:, 4:8], AF.Exp,
                                    scale=-0.5)
            step("A", act_rstd, "c_rs")
            for st in range(4):
                g = 4 * T + st
                par = g % 4
                wait("V", "c_rs", led["c_rs"])
                if g >= 2:
                    wait("V", "c_tp", 2 * (g - 1))
                step("V", lambda e, g=g, par=par: e.tensor_scalar(
                    xn_t[g % 2][:, :], stg[par][:, :],
                    means_a[:, g:g + 1], rstds_a[:, g:g + 1],
                    op0=ALU.subtract, op1=ALU.mult), "c_ln")
                for half in range(2):
                    wait("T", "c_ln", led["c_ln"])
                    wait("T", "c_te", pdt_evac[half])
                    wait("T", "c_ve", 4 + T)

                    def trs(e, g=g, half=half):
                        last = None
                        for j in range(8):
                            f = 8 * half + j
                            last = e.transpose(
                                PDT[:, 1024 * half + 128 * j:
                                    1024 * half + 128 * (j + 1)],
                                xn_t[g % 2][:, 128 * f:128 * (f + 1)], ident[:, :])
                        return last
                    step("T", trs, "c_tp")
                    wait("V", "c_tp", led["c_tp"])
                    step("V", lambda e, st=st, half=half: e.tensor_copy(
                        xnT_cur[:, 8 * half:8 * (half + 1), 128 * st:128 * (st + 1)],
                        PDT[:, 1024 * half:1024 * (half + 1)].rearrange(
                            "p (j c) -> p j c", j=8)), "c_te")
                    pdt_evac[half] = led["c_te"]
            if T == 0:
                wait("T", "dwq", 16)
            for cc in range(6):
                bank = cc % 4
                wait("T", "c_te", led["c_te"])
                wait("T", "c_qe", qkb_uses[bank])

                def qkv_mm(e, cc=cc, bank=bank):
                    last = None
                    for f in range(16):
                        last = e.matmul(qk_banks[bank],
                                        wqkv_sb[:, f, 128 * cc:128 * (cc + 1)],
                                        xnT_cur[:, f, :],
                                        start=(f == 0), stop=(f == 15))
                    return last
                step("T", qkv_mm, "c_qk")
                wait("A", "c_qk", led["c_qk"])
                if cc >= 4 and T > 0:
                    wait("A", "c_vt", led["c_vt"])

                def qkv_ev(e, T=T, cc=cc, bank=bank):
                    if cc < 2:
                        dst = QT[cc][:, 512 * T:512 * (T + 1)]
                    elif cc < 4:
                        dst = KT[cc - 2][:, 512 * T:512 * (T + 1)]
                    else:
                        dst = vtmp_t[cc - 4][:, :]
                    return e.activation(dst, qk_banks[bank], AF.Identity,
                                        bias=bias_qkv[:, cc:cc + 1])
                step("A", qkv_ev, "c_qe")
                qkb_uses[bank] = led["c_qe"]
            for cc in range(4):
                tile = QT[cc] if cc < 2 else KT[cc - 2]
                half = cc % 2
                wait("T", "c_qe", 6 * T + cc + 1)
                wait("T", "c_se", swp_uses[half])
                step("T", lambda e, tile=tile, T=T, half=half: e.matmul(
                    PC[:, 512 * half:512 * (half + 1)], swapm[:, :],
                    tile[:, 512 * T:512 * (T + 1)], start=True, stop=True), "c_sw")
                wait("A", "c_sw", led["c_sw"])
                wait("A", "c_ro", swp_rope[half])
                step("A", lambda e, half=half: e.activation(
                    swp_t[half][:, :], PC[:, 512 * half:512 * (half + 1)], AF.Copy),
                    "c_se")
                swp_uses[half] = led["c_se"]
                wait("V", "c_se", led["c_se"])
                wait("V", "dcs%d" % (T % 2), led["dcs%d" % (T % 2)])

                def rope(e, tile=tile, T=T, half=half, par=T % 2):
                    sl = tile[:, 512 * T:512 * (T + 1)]
                    e.tensor_mul(rope1[:, :], sl, cs1_t[par][:, :])
                    e.tensor_mul(rope2[:, :], swp_t[half][:, :], cs2_t[par][:, :])
                    return e.tensor_add(sl, rope1[:, :], rope2[:, :])
                step("V", rope, "c_ro")
                swp_rope[half] = led["c_ro"]
            cs_ro[T % 2] = led["c_ro"]
            if T + 1 < nt:
                par = (T + 1) % 2
                wait("Y", "c_ro", cs_ro[par])
                step("Y", lambda e, T=T: e.dma_start(
                    out=cs1_t[(T + 1) % 2][:, :],
                    in_=d_cs1.ap()[:, 512 * (T + 1):512 * (T + 2)]), "dcs%d" % par, 16)
                step("Y", lambda e, T=T: e.dma_start(
                    out=cs2_t[(T + 1) % 2][:, :],
                    in_=d_cs2.ap()[:, 512 * (T + 1):512 * (T + 2)]), "dcs%d" % par, 16)
            for vv in range(2):
                wait("T", "c_qe", 6 * T + 5 + vv)
                wait("T", "c_te", led["c_te"])
                if T > 0:
                    wait("T", "c_ve", 4 + T)

                def vtr(e, vv=vv):
                    last = None
                    for j in range(4):
                        last = e.transpose(
                            PDT[:, 1024 * vv + 128 * j:1024 * vv + 128 * (j + 1)],
                            vtmp_t[vv][:, 128 * j:128 * (j + 1)], ident[:, :])
                    return last
                step("T", vtr, "c_vt")
            wait("V", "c_vt", led["c_vt"])

            def vev(e, T=T):
                last = None
                for vv in range(2):
                    for j in range(4):
                        tc = 4 * T + j
                        for a in range(2):
                            last = e.tensor_copy(
                                V65[2 * vv + a][:, 65 * tc:65 * tc + 64],
                                PDT[:, 1024 * vv + 128 * j + 64 * a:
                                    1024 * vv + 128 * j + 64 * (a + 1)])
                return last
            step("V", vev, "c_ve")
        if "XS" in tap_t:
            run("G", lambda e: e.wait_ge(sems["c_ve"], led["c_ve"]))
            run("G", lambda e: e.wait_ge(sems["c_ro"], led["c_ro"]))
            step("G", lambda e: e.dma_start(out=tap_t["XS"].ap(), in_=x_stage[1][:, :]),
                 "dtap", 16)
            step("G", lambda e: e.dma_start(out=tap_t["XN"].ap(), in_=xn_t[1][:, :]),
                 "dtap", 16)
            step("G", lambda e: e.dma_start(out=tap_t["ST"].ap(), in_=stats[:, :]),
                 "dtap", 16)
            step("G", lambda e: e.dma_start(
                out=tap_t["XT"].ap(),
                in_=xnT_cur[:, :, :].rearrange("p f c -> p (f c)")), "dtap", 16)
        if "QT0" in tap_t:
            run("G", lambda e: e.wait_ge(sems["c_ro"], led["c_ro"]))
            run("G", lambda e: e.wait_ge(sems["c_ve"], led["c_ve"]))
            step("G", lambda e: e.dma_start(out=tap_t["QT0"].ap(), in_=QT[0][:, :]),
                 "dtap", 16)
            step("G", lambda e: e.dma_start(out=tap_t["KT1"].ap(), in_=KT[1][:, :]),
                 "dtap", 16)
            step("G", lambda e: e.dma_start(out=tap_t["V0"].ap(), in_=V65[0][:, :]),
                 "dtap", 16)
            step("G", lambda e: e.dma_start(out=tap_t["QT1"].ap(), in_=QT[1][:, :]),
                 "dtap", 16)
            step("G", lambda e: e.dma_start(out=tap_t["KT0"].ap(), in_=KT[0][:, :]),
                 "dtap", 16)
            step("G", lambda e: e.dma_start(out=tap_t["V2"].ap(), in_=V65[2][:, :]),
                 "dtap", 16)

    def p2():
        step("Y", lambda e: e.dma_start(
            out=masks_sb[:, :, :],
            in_=d_masks.ap().rearrange("v p c -> p v c")), "dmsk", 16)
        wait("A", "c_rl", 1)
        sb_map = [PA, PB, PD32]
        sb_last_exp = [0, 0, 0]
        strip_last_pv = [0, 0, 0]
        o_last_evac = [0, 0]
        units = []
        for h in range(4):
            for q in range(NT):
                npair = 2 * (q + 1)
                for p in range(npair):
                    units.append((h, q, p, npair))
        DEPTH = 2

        def emit_pv(u):
            h, q, p, npair = units[u]
            v = V65[h]
            ooff = 512 * ((h * NT + q) % 2)
            strip = pstrip[u % 3]
            wait("T", "c_ex", u + 1)

            def pv(e, v=v, p=p, strip=strip, ooff=ooff,
                   first=(p == 0), last=(p == npair - 1)):
                e.matmul(PC[0:65, ooff:ooff + 512],
                         v[:, 65 * 2 * p:65 * 2 * p + 65],
                         strip[:, 0:512], start=first, stop=False)
                return e.matmul(PC[0:65, ooff:ooff + 512],
                                v[:, 65 * (2 * p + 1):65 * (2 * p + 1) + 65],
                                strip[:, 512:1024], start=False, stop=last)
            step("T", pv, "c_pv")
            strip_last_pv[u % 3] = led["c_pv"]
            if p == npair - 1:
                wait("V", "c_pv", led["c_pv"])
                lpar = (h * NT + q) % 2
                if led["c_lx"] >= 2:
                    wait("V", "dlx", 16 * (led["c_lx"] - 1))
                r0 = 64 * (h % 2)
                step("V", lambda e, ooff=ooff, lpar=lpar: e.tensor_copy(
                    stage_l[lpar][:, :], PC[64:65, ooff:ooff + 512]), "c_lx")
                step("V", lambda e, h=h, q=q, r0=r0, ooff=ooff: e.tensor_copy(
                    A_t[h // 2][r0:r0 + 64, q // 2,
                                512 * (q % 2):512 * (q % 2) + 512],
                    PC[0:64, ooff:ooff + 512]), "c_oe")
                wait("Y", "c_lx", led["c_lx"])
                step("Y", lambda e, h=h, q=q, lpar=lpar: e.dma_start(
                    out=l_buf[16 * h + q:16 * h + q + 1, :],
                    in_=stage_l[lpar][:, :]), "dlx", 16)
                o_last_evac[(h * NT + q) % 2] = led["c_oe"]

        for u, (h, q, p, npair) in enumerate(units):
            qt, kt = QT[h // 2], KT[h // 2]
            r0 = 64 * (h % 2)
            sb = sb_map[u % 3]
            strip = pstrip[u % 3]
            if p == 0:
                wait("T", "c_oe", o_last_evac[(h * NT + q) % 2])
            wait("T", "c_ex", sb_last_exp[u % 3])

            masked = (p >= 2 * q)
            if masked and u < 3:
                wait("T", "dmsk", 16)

            def smm2(e, sb=sb, kt=kt, qt=qt, r0=r0, p=p, q=q, masked=masked):
                voff = 2 * (p - 2 * q)
                e.matmul(sb[:, 0:512],
                         kt[r0:r0 + 64, 256 * p:256 * p + 128],
                         qt[r0:r0 + 64, 512 * q:512 * (q + 1)],
                         start=True, stop=not masked)
                if masked:
                    e.matmul(sb[:, 0:512], ident[:, :], masks_sb[:, voff, :],
                             start=False, stop=True)
                last = e.matmul(sb[:, 512:1024],
                                kt[r0:r0 + 64, 256 * p + 128:256 * (p + 1)],
                                qt[r0:r0 + 64, 512 * q:512 * (q + 1)],
                                start=True, stop=not masked)
                if masked:
                    last = e.matmul(sb[:, 512:1024], ident[:, :],
                                    masks_sb[:, voff + 1, :],
                                    start=False, stop=True)
                return last
            step("T", smm2, "c_ss")
            wait("A", "c_ss", led["c_ss"])
            wait("A", "c_pv", strip_last_pv[u % 3])
            step("A", lambda e, strip=strip, sb=sb: e.activation(
                strip[:, :], sb[:, :], AF.Exp,
                bias=ebias[:, 0:1], scale=EXP_SCALE), "c_ex")
            sb_last_exp[u % 3] = led["c_ex"]
            if u >= DEPTH:
                emit_pv(u - DEPTH)
        for u in range(len(units) - DEPTH, len(units)):
            emit_pv(u)
        wait("V", "c_oe", led["c_oe"])
        wait("V", "dlx", led["dlx"])
        vstep(lambda e: e.reciprocal(l_buf[:, :], l_buf[:, :]))
        step("V", lambda e: e.tensor_copy(rl_buf[:, :], l_buf[:, :]), "c_rl")
        if "A0" in tap_t:
            run("G", lambda e: e.wait_ge(sems["c_rl"], led["c_rl"]))
            step("G", lambda e: e.dma_start(
                out=tap_t["A0"].ap(),
                in_=A_t[0][:, :, :].rearrange("p j c -> p (j c)")), "dtap", 16)
            step("G", lambda e: e.dma_start(
                out=tap_t["A1"].ap(),
                in_=A_t[1][:, :, :].rearrange("p j c -> p (j c)")), "dtap", 16)
            step("G", lambda e: e.dma_start(out=tap_t["L"].ap(), in_=l_buf[:, :]),
                 "dtap", 16)

    def p25():
        for tt in range(2):
            step("Y", lambda e, tt=tt: e.dma_start(
                out=d_a2a_in.ap().rearrange("(j t p) c -> t p j c",
                                            j=8, t=2, p=128)[tt],
                in_=A_t[tt][:, :, :]), "dst", 16)
        for lh in range(4):
            step("Y", lambda e, lh=lh: e.dma_start(
                out=d_rl_in.ap().rearrange("(j lh jj) c -> lh j jj c",
                                           j=8, lh=4, jj=2)[lh],
                in_=rl_buf[16 * lh:16 * (lh + 1), :]), "dst", 16)
        wait("G", "dst", led["dst"])

        def cc1(e):
            with allow_ici_all_to_all():
                return e.collective_compute(
                    "AllToAll", ALU.bypass, replica_groups=[list(range(NCORE))],
                    ins=[d_a2a_in.ap().opt()], outs=[d_a2a_out.ap().opt()])
        step("G", cc1, "c_cc")

        def cc2(e):
            with allow_ici_all_to_all():
                return e.collective_compute(
                    "AllToAll", ALU.bypass, replica_groups=[list(range(NCORE))],
                    ins=[d_rl_in.ap().opt()], outs=[d_rl_out.ap().opt()])
        step("G", cc2, "c_cc")

    def p3_xn_own():
        for g in range(8):
            par = g % 2
            if g >= 2:
                wait("Y", "c_sq", led["c_sq"] - 1)
                wait("Y", "c_ln", led["c_ln"] - 1)
            step("Y", lambda e, g=g, par=par: e.dma_start(
                out=xo_stage[par][:, :], in_=d_xo.ap()[128 * g:128 * (g + 1), :]),
                "dxo%d" % par, 16)
            step("Y", lambda e, g=g, par=par: e.dma_start(
                out=Y_sb[:, g, :], in_=d_xo.ap()[128 * g:128 * (g + 1), :]),
                "dxo%d" % par, 16)
            wait("A", "dxo%d" % par, led["dxo%d" % par])
            if g >= 2:
                wait("A", "c_ln", led["c_ln"] - 1)
                wait("A", "c_tp", led["c_tp"] - 2)
            step("A", lambda e, par=par: e.activation(
                xn_to[par][:, :], xo_stage[par][:, :], AF.Square,
                accum_out=stats[:, 4 + par:5 + par]), "c_sq")
            wait("V", "dxo%d" % par, led["dxo%d" % par])
            vstep(lambda e, par=par: e.reduce_sum(
                stats[:, 0 + par:1 + par], xo_stage[par][:, :], axis=AX))
            wait("V", "c_sq", led["c_sq"])
            vstep(lambda e, par=par: e.tensor_scalar(
                stats[:, 6 + par:7 + par], stats[:, 0 + par:1 + par],
                1.0 / D, None, op0=ALU.mult))
            vstep(lambda e, par=par: e.tensor_scalar(
                stats[:, 8 + par:9 + par], stats[:, 4 + par:5 + par],
                1.0 / D, None, op0=ALU.mult))
            vstep(lambda e, par=par: e.tensor_mul(
                stats[:, 10 + par:11 + par], stats[:, 6 + par:7 + par],
                stats[:, 6 + par:7 + par]))
            vstep(lambda e, par=par: e.tensor_sub(
                stats[:, 8 + par:9 + par], stats[:, 8 + par:9 + par],
                stats[:, 10 + par:11 + par]))
            step("V", lambda e, par=par: e.tensor_scalar(
                stats[:, 8 + par:9 + par], stats[:, 8 + par:9 + par],
                EPS, None, op0=ALU.add), "c_vr")
            wait("A", "c_vr", led["c_vr"])

            def act_rstd(e, par=par):
                e.activation(stats[:, 14:15], stats[:, 8 + par:9 + par], AF.Ln)
                return e.activation(stats[:, 12 + par:13 + par], stats[:, 14:15],
                                    AF.Exp, scale=-0.5)
            step("A", act_rstd, "c_rs")
            wait("V", "c_rs", led["c_rs"])
            step("V", lambda e, par=par: e.tensor_scalar(
                xn_to[par][:, :], xo_stage[par][:, :],
                stats[:, 6 + par:7 + par], stats[:, 12 + par:13 + par],
                op0=ALU.subtract, op1=ALU.mult), "c_ln")
            for half in range(2):
                wait("T", "c_ln", led["c_ln"])
                if g > 0:
                    wait("T", "c_te", led["c_te"] - 1)

                def trs(e, par=par, half=half):
                    last = None
                    for j in range(8):
                        f = 8 * half + j
                        last = e.transpose(
                            PDT[:, 1024 * half + 128 * j:1024 * half + 128 * (j + 1)],
                            xn_to[par][:, 128 * f:128 * (f + 1)], ident[:, :])
                    return last
                step("T", trs, "c_tp")
                wait("V", "c_tp", led["c_tp"])
                if g == 0 and half == 0:
                    wait("V", "dst", led["dst"])
                step("V", lambda e, g=g, half=half: e.tensor_copy(
                    xn_ownT[:, 8 * half:8 * (half + 1), 128 * g:128 * (g + 1)],
                    PDT[:, 1024 * half:1024 * (half + 1)].rearrange(
                        "p (j c) -> p j c", j=8)), "c_te")

    def p3_router():
        step("Y", lambda e: e.dma_start(out=bout_sb[:, :], in_=d_bout.ap()),
             "dmisc", 16)
        wait("T", "dmisc", led["dmisc"])
        wait("T", "c_ex", led["c_ex"])

        def bb(e):
            last = None
            tgts = [PA[:, 0:512], PA[:, 512:1024], PB[:, 0:512], PB[:, 512:1024]]
            for j in range(4):
                last = e.matmul(tgts[j], ones1[:, :],
                                bout_sb[:, 512 * j:512 * (j + 1)],
                                start=True, stop=True)
            return last
        step("T", bb, "c_bc")
        wait("V", "c_bc", led["c_bc"])

        def bbev(e):
            e.tensor_copy(bout_b[:, 0:1024], PA[:, :])
            return e.tensor_copy(bout_b[:, 1024:2048], PB[:, :])
        step("V", bbev, "c_be")
        wait("T", "c_te", led["c_te"])
        wait("T", "c_be", led["c_be"])
        for th in range(2):
            def lg(e, th=th):
                last = None
                for f in range(16):
                    last = e.matmul(PA[0:8, 512 * th:512 * (th + 1)],
                                    wgate_sb[:, f, :],
                                    xn_ownT[:, f, 512 * th:512 * (th + 1)],
                                    start=(f == 0), stop=(f == 15))
                return last
            step("T", lg, "c_rg")
        wait("A", "c_rg", led["c_rg"])
        step("A", lambda e: e.activation(logits_sb[:, :], PA[0:8, :], AF.Copy),
             "c_rte")
        wait("T", "c_rte", led["c_rte"])

        def ltr(e):
            last = None
            for j in range(8):
                last = e.transpose(PB[:, 8 * j:8 * (j + 1)],
                                   logits_sb[:, 128 * j:128 * (j + 1)],
                                   ident32[0:8, 0:8])
            return last
        step("T", ltr, "c_rtr")
        wait("A", "c_rtr", led["c_rtr"])
        step("A", lambda e: e.activation(e_sb[:, :], PB[:, 0:64], AF.Exp), "c_rex")
        wait("V", "c_rex", led["c_rex"])
        for j in range(8):
            es = e_sb[:, 8 * j:8 * (j + 1)]
            ca = comb_all[:, 8 * j:8 * (j + 1)]
            m1, m2, t_ = stats[:, 0:1], stats[:, 1:2], stats[:, 2:3]
            vstep(lambda e, es=es, m1=m1: e.reduce_max(m1, es, axis=AX))
            vstep(lambda e, es=es, ca=ca, m1=m1: e.tensor_scalar(
                ca, es, m1, None, op0=ALU.is_ge))
            tmp8 = stats[:, 8:16]
            vstep(lambda e, es=es, ca=ca, tmp8=tmp8: e.tensor_mul(tmp8, ca, es))
            vstep(lambda e, es=es, ca=ca, tmp8=tmp8: e.tensor_sub(ca, es, tmp8))
            vstep(lambda e, ca=ca, m2=m2: e.reduce_max(m2, ca, axis=AX))
            vstep(lambda e, es=es, ca=ca, m2=m2: e.tensor_scalar(
                ca, es, m2, None, op0=ALU.is_ge))
            vstep(lambda e, es=es, ca=ca: e.tensor_mul(ca, ca, es))
            vstep(lambda e, m1=m1, m2=m2, t_=t_: e.tensor_add(t_, m1, m2))
            vstep(lambda e, t_=t_: e.reciprocal(t_, t_))
            vstep(lambda e, ca=ca, t_=t_: e.tensor_scalar(
                ca, ca, t_, None, op0=ALU.mult))
        step("V", lambda e: e.engine_nop(), "c_rv")
        wait("T", "c_rv", led["c_rv"])

        def ctr(e):
            last = None
            for j in range(8):
                last = e.transpose(PC[0:8, 128 * j:128 * (j + 1)],
                                   comb_all[:, 8 * j:8 * (j + 1)], ident32[:, :])
            return last
        step("T", ctr, "c_rt2")
        wait("V", "c_rt2", led["c_rt2"])
        step("V", lambda e: e.tensor_copy(combine_T[:, :], PC[0:8, :]), "c_rce")
        wait("V", "dxo0", led["dxo0"])
        wait("V", "dxo1", led["dxo1"])

        def yinit(e):
            last = None
            for c in range(8):
                last = e.tensor_add(Y_sb[:, c, :], Y_sb[:, c, :], bout_b[:, :])
            return last
        step("V", yinit, "c_ya")
        if "LG" in tap_t:
            run("G", lambda e: e.wait_ge(sems["c_rce"], led["c_rce"]))
            step("G", lambda e: e.dma_start(out=tap_t["LG"].ap(), in_=logits_sb[:, :]),
                 "dtap", 16)
            step("G", lambda e: e.dma_start(out=tap_t["ES"].ap(), in_=e_sb[:, :]),
                 "dtap", 16)
            step("G", lambda e: e.dma_start(out=tap_t["CA"].ap(), in_=comb_all[:, :]),
                 "dtap", 16)
            step("G", lambda e: e.dma_start(out=tap_t["CT2"].ap(), in_=combine_T[:, :]),
                 "dtap", 16)
        run("A", lambda e: load_act_set(e, 18))
        if "CT" in tap_t:
            run("G", lambda e: e.wait_ge(sems["c_rce"], led["c_rce"]))
            step("G", lambda e: e.dma_start(out=tap_t["CT"].ap(),
                                            in_=combine_T[:, :]), "dtap", 16)

    gu_chunk = [0]
    dn_chunk = [0]
    gu_dma = [0, 0]
    dn_dma = [0, 0]
    gu_targets = []
    dn_targets = []

    def ffn_weights(kind, e_idx, first):
        for i in range(8):
            n = gu_chunk[0]
            par = n % 2
            if first and i < 2:
                wait("G", "c_te", led["c_te"])
                wait("G", "c_rce", led["c_rce"])
                wait("G", "dst", led["dst"])
            if n >= 2:
                wait("G", "c_gu", n - 1)
            src = (d_wsg.ap()[e_idx, i] if kind == "sh" else d_wgu.ap()[e_idx, i])
            step("G", lambda e, src=src, par=par: e.dma_start(
                out=sgw[par][:, :, :],
                in_=src.rearrange("f p c -> p f c")), "dgu%d" % par, 16)
            gu_dma[par] += 1
            gu_targets.append(16 * gu_dma[par])
            gu_chunk[0] += 1
        for hc in range(4):
            m = dn_chunk[0]
            par = m % 2
            if first and hc < 2:
                wait("G", "c_te", led["c_te"])
                if par == 1:
                    wait("G", "c_bc", 1)
            if m >= 2:
                wait("G", "c_dn", 8 * (m - 1))
            src = (d_wsd.ap()[e_idx, hc] if kind == "sh" else d_wdn.ap()[e_idx, hc])
            step("G", lambda e, src=src, par=par: e.dma_start(
                out=sdw[par][:, :, :],
                in_=src.rearrange("i p c -> p i c")), "ddn%d" % par, 16)
            dn_dma[par] += 1
            dn_targets.append(16 * dn_dma[par])
            dn_chunk[0] += 1

    gu_use = [0]
    dn_use = [0]

    def ffn_block(kind, e_idx):
        ya_start = led["c_ya"]
        if kind == "ex":
            wait("Y", "c_rce", led["c_rce"])
            wait("Y", "c_bc", led["c_bc"])
            step("Y", lambda e, ex=e_idx: e.dma_start(
                out=comb_stage[:, :], in_=combine_T[ex:ex + 1, :]), "dcb", 16)
            wait("T", "dcb", led["dcb"])
            wait("T", "c_am", led["c_am"])

            def cb(e):
                e.matmul(PD32[:, 0:512], ones1[:, :], comb_stage[0:1, 0:512],
                         start=True, stop=True)
                return e.matmul(PD32[:, 512:1024], ones1[:, :],
                                comb_stage[0:1, 512:1024],
                                start=True, stop=True)
            step("T", cb, "c_bc")
            wait("V", "c_bc", led["c_bc"])
            step("V", lambda e: e.tensor_copy(c_bcast[:, :], PD32[:, :]), "c_be")
        for i in range(8):
            n = gu_use[0]
            par = n % 2
            wait("T", "dgu%d" % par, gu_targets[n])
            gu_use[0] += 1
            gb, ub = (PA, PB) if par == 0 else (PC, PD32)
            if i == 0:
                wait("T", "c_ya", ya_start)
                wait("T", "c_rt2", led["c_rt2"])
            wait("T", "c_am", max(0, led["c_am"] - 1))
            if kind == "ex":
                wait("T", "c_be", led["c_be"])

            def gumm(e, par=par, gb=gb, ub=ub):
                last = None
                for f in range(16):
                    for th in range(2):
                        e.matmul(gb[:, 512 * th:512 * (th + 1)],
                                 sgw[par][:, f, 0:128],
                                 xn_ownT[:, f, 512 * th:512 * (th + 1)],
                                 start=(f == 0), stop=(f == 15))
                    for th in range(2):
                        last = e.matmul(ub[:, 512 * th:512 * (th + 1)],
                                        sgw[par][:, f, 128:256],
                                        xn_ownT[:, f, 512 * th:512 * (th + 1)],
                                        start=(f == 0), stop=(f == 15))
                return last
            step("T", gumm, "c_gu")
            wait("A", "c_gu", led["c_gu"])
            if led["c_am"] >= 2:
                wait("A", "c_am", led["c_am"] - 1)
            step("A", lambda e, gb=gb, par=par: e.activation(
                sg_t[par][:, :], gb[:, :], AF.Silu), "c_si")
            wait("V", "c_si", led["c_si"])
            if kind == "ex":
                vstep(lambda e, ub=ub, par=par, i=i: e.tensor_mul(
                    act_sb[:, i, :], sg_t[par][:, :], ub[:, :]))
                step("V", lambda e, i=i: e.tensor_mul(
                    act_sb[:, i, :], act_sb[:, i, :], c_bcast[:, :]), "c_am")
            else:
                step("V", lambda e, ub=ub, par=par, i=i: e.tensor_mul(
                    act_sb[:, i, :], sg_t[par][:, :], ub[:, :]), "c_am")
        for hc in range(4):
            m = dn_use[0]
            par = m % 2
            wait("T", "ddn%d" % par, dn_targets[m])
            dn_use[0] += 1
            if hc == 0:
                wait("T", "c_am", led["c_am"])
            for tc in range(8):
                dpar = tc % 2
                db = PA[:, 512 * dpar:512 * (dpar + 1)]
                wait("T", "c_ya", max(ya_start, led["c_ya"] - 1))

                def dmm(e, db=db, tc=tc, par=par):
                    last = None
                    for p in range(8):
                        last = e.matmul(db,
                                        act_sb[:, p, 128 * tc:128 * (tc + 1)],
                                        sdw[par][:, p, :],
                                        start=(p == 0), stop=(p == 7))
                    return last
                step("T", dmm, "c_dn")
                wait("V", "c_dn", led["c_dn"])
                step("V", lambda e, db=db, tc=tc, hc=hc: e.tensor_add(
                    Y_sb[:, tc, 512 * hc:512 * (hc + 1)],
                    Y_sb[:, tc, 512 * hc:512 * (hc + 1)], db), "c_ya")

    def p5(parts='rdwy'):
        wait("Y", "c_cc", 2)
        wait("Y", "c_ya", led["c_ya"])
        step("Y", lambda e: e.dma_start(
            out=AT_rb[:, :, :],
            in_=d_a2a_out.ap().rearrange("(f p) c -> p f c", p=128)), "drb", 16)
        step("Y", lambda e: e.dma_start(
            out=rl_sb[:, :, :].rearrange("r (k2 k1) (jj c) -> r k2 k1 jj c",
                                         k2=8, jj=2),
            in_=d_rl_out.ap().rearrange("(k2 k1 r jj) c -> r k2 k1 jj c",
                                        k2=8, k1=2, r=2, jj=2)), "drb", 16)
        if 'w' not in parts:
            return
        wob_targets = []
        for cc2 in range(4):
            par = cc2 % 2
            if cc2 < 2:
                wait("G", "c_ya", led["c_ya"])
            else:
                wait("G", "c_wo", 8 * (cc2 - 1))
            step("G", lambda e, cc2=cc2, par=par: e.dma_start(
                out=wob[par][:, :, :],
                in_=d_wout.ap()[cc2].rearrange("(f p) c -> p f c", p=128)),
                "dgu%d" % par, 16)
            wob_targets.append(led["dgu%d" % par])
        if 'd' not in parts:
            return
        wait("T", "drb", 32)
        wait("T", "c_ya", led["c_ya"])
        for k in range(16):
            def dbc(e, k=k):
                e.matmul(PB[:, 0:512], sel2[:, :], rl_sb[:, k, 0:512],
                         start=True, stop=True)
                return e.matmul(PB[:, 512:1024], sel2[:, :], rl_sb[:, k, 512:1024],
                                start=True, stop=True)
            step("T", dbc, "c_bc")
            wait("V", "c_bc", led["c_bc"])
            step("V", lambda e, k=k: e.tensor_mul(
                AT_rb[:, k, :], AT_rb[:, k, :], PB[:, :]), "c_dv")
            wait("T", "c_dv", led["c_dv"])
        wo_n = 0
        for cc2 in range(4):
            par = cc2 % 2
            wait("T", "dgu%d" % par, wob_targets[cc2])
            for tc in range(8):
                dpar = tc % 2
                db = PA[:, 512 * dpar:512 * (dpar + 1)]
                wait("T", "c_ya", led["c_ya"] - 1)

                def wmm(e, db=db, tc=tc, par=par):
                    last = None
                    for f in range(16):
                        last = e.matmul(db, AT_rb[:, f, 128 * tc:128 * (tc + 1)],
                                        wob[par][:, f, :],
                                        start=(f == 0), stop=(f == 15))
                    return last
                step("T", wmm, "c_wo")
                wo_n += 1
                wait("V", "c_wo", led["c_wo"])
                step("V", lambda e, db=db, tc=tc, cc2=cc2: e.tensor_add(
                    Y_sb[:, tc, 512 * cc2:512 * (cc2 + 1)],
                    Y_sb[:, tc, 512 * cc2:512 * (cc2 + 1)], db), "c_ya")
        if "RB" in tap_t:
            run("G", lambda e: e.wait_ge(sems["c_dv"], led["c_dv"]))
            step("G", lambda e: e.dma_start(
                out=tap_t["RB"].ap(),
                in_=AT_rb[:, :, :].rearrange("p f c -> p (f c)")), "dtap", 16)
        if 'y' not in parts:
            return
        wait("Y", "c_ya", led["c_ya"])
        for c in range(8):
            step("Y", lambda e, c=c: e.dma_start(
                out=d_y.ap()[128 * c:128 * (c + 1), :], in_=Y_sb[:, c, :]),
                "dyo", 16)
        if tap_t:
            wait("Y", "dtap", led["dtap"])
        wait("Y", "dyo", led["dyo"])

    p0()
    p1()
    if phases >= 2:
        barrier()
        p2()
        barrier()
        p25()
    if phases >= 3:
        p3_xn_own()
        p3_router()
    if phases >= 4:
        first = True
        if "sh" not in skip:
            for qq in range(4):
                ffn_weights("sh", qq, first=first)
                ffn_block("sh", qq)
                first = False
        if "ex" not in skip:
            for ex in range(8):
                ffn_weights("ex", ex, first=first)
                ffn_block("ex", ex)
                first = False
    if phases >= 5:
        p5(p5_parts)
    else:
        if tap_t:
            wait("Y", "dtap", led["dtap"])
        for k in "TVAYG":
            plan[k].append(lambda e, ss=sems["bar"]: e.nop().then_inc(ss, 1))
        wait("Y", "bar", led["bar"] + 5)
        led["bar"] += 5
        step("Y", lambda e: e.dma_start(out=d_y.ap()[0:128, :], in_=x_stage[0][:, :]),
             "dyo", 16)
        wait("Y", "dyo", led["dyo"])

    with nc.Block() as blk:
        @blk.tensor
        def _(e):
            for f in plan["T"]:
                f(e)

        @blk.vector
        def _(e):
            for f in plan["V"]:
                f(e)

        @blk.scalar
        def _(e):
            for f in plan["A"]:
                f(e)

        @blk.sync
        def _(e):
            for f in plan["Y"]:
                f(e)

        @blk.gpsimd
        def _(e):
            for f in plan["G"]:
                f(e)

    return nc


def _host_consts():
    bf = ml_dtypes.bfloat16
    inv_freq = (1.0 / 10000.0 ** (np.arange(0, 32, 2, dtype=np.float64) / 32.0))
    t = np.arange(S, dtype=np.float64)
    freqs = np.outer(t, inv_freq)
    cos = np.cos(freqs)
    sin = np.sin(freqs)
    cs1 = np.zeros((128, S), np.float32)
    cs2 = np.zeros((128, S), np.float32)
    for r in range(128):
        d = r % 64
        if d < 16:
            cs1[r] = cos[:, d]
            cs2[r] = -sin[:, d]
        elif d < 32:
            cs1[r] = cos[:, d - 16]
            cs2[r] = sin[:, d - 16]
        else:
            cs1[r] = 1.0
            cs2[r] = 0.0
    swap = np.zeros((128, 128), np.float32)
    for b in (0, 64):
        for i in range(16):
            swap[b + 16 + i, b + i] = 1.0
            swap[b + i, b + 16 + i] = 1.0
    masks = np.zeros((4, 128, 512), np.float32)
    for v in range(4):
        i = np.arange(128)[:, None]
        j = np.arange(512)[None, :]
        masks[v] = np.where(128 * v + i > j, MASK_NEG, 0.0)
    sel2 = np.zeros((2, 128), np.float32)
    sel2[0, 0:64] = 1.0
    sel2[1, 64:128] = 1.0
    return {
        "cs1": cs1.astype(bf), "cs2": cs2.astype(bf),
        "masks": masks.astype(bf),
        "identb": np.eye(128, dtype=bf),
        "ident32": np.eye(128, dtype=np.float32),
        "swapm": swap.astype(bf),
        "sel2": sel2.astype(bf),
        "ones1": np.ones((1, 128), np.float32),
    }


def _prep_inputs(inputs):
    f = np.float32
    x = np.ascontiguousarray(np.asarray(inputs["L0_attn_x"], f).reshape(S, D))
    Wqkv = np.asarray(inputs["L0_attn_Wqkv"], f)
    bqkv = np.asarray(inputs["L0_attn_bqkv"], f)
    Wout = np.asarray(inputs["L0_attn_Wout"], f)
    bout = np.asarray(inputs["L0_attn_bout"], f)
    wg = np.asarray(inputs["L0_ffn_w_gate"], f)
    wgu = np.asarray(inputs["L0_ffn_w_gate_up"], f)
    wdn = np.asarray(inputs["L0_ffn_w_down"], f)
    wsg = np.asarray(inputs["L0_ffn_w_sh_gate_up"], f)
    wsd = np.asarray(inputs["L0_ffn_w_sh_down"], f)

    consts = _host_consts()
    shared = dict(consts)
    shared["x"] = x
    shared["wgate"] = np.ascontiguousarray(wg)
    shared["bout"] = np.ascontiguousarray(bout.reshape(1, D))
    shared["wout"] = np.ascontiguousarray(
        Wout.reshape(D, 4, 512).transpose(1, 0, 2))
    g = wgu[:, :, 0:1024].reshape(8, 16, 128, 8, 128)
    u = wgu[:, :, 1024:2048].reshape(8, 16, 128, 8, 128)
    gu = np.concatenate([g, u], axis=4)
    shared["wgu"] = np.ascontiguousarray(gu.transpose(0, 3, 1, 2, 4))
    shared["wdn"] = np.ascontiguousarray(
        wdn.reshape(8, 8, 128, 4, 512).transpose(0, 3, 1, 2, 4))
    gs = wsg[:, 0:4096].reshape(16, 128, 32, 128)
    us = wsg[:, 4096:8192].reshape(16, 128, 32, 128)
    gus = np.concatenate([gs, us], axis=3)
    shared["wsg"] = np.ascontiguousarray(
        gus.transpose(2, 0, 1, 3).reshape(4, 8, 16, 128, 256))
    shared["wsd"] = np.ascontiguousarray(
        wsd.reshape(4, 8, 128, 4, 512).transpose(0, 3, 1, 2, 4))

    in_maps = []
    for c in range(NCORE):
        m = dict(shared)
        m["x_own"] = np.ascontiguousarray(x[TOK * c:TOK * (c + 1)])
        cols = []
        bc = []
        for sec in range(3):
            lo = 2048 * sec + 256 * c
            cols.append(Wqkv[:, lo:lo + 256])
            bc.append(bqkv[lo:lo + 256])
        m["wqkv"] = np.ascontiguousarray(np.concatenate(cols, axis=1))
        bqkv_t = np.concatenate(bc)
        m["bqkv"] = np.ascontiguousarray(
            bqkv_t.reshape(6, 128).T)
        in_maps.append(m)
    return in_maps


def kernel(**inputs):
    nc = build()
    in_maps = _prep_inputs(inputs)
    res = run_bass_kernel_spmd(nc, in_maps, core_ids=list(range(NCORE)))
    y = np.concatenate([res.results[c]["y"] for c in range(NCORE)], axis=0)
    return y.reshape(1, S, D).astype(np.float32)


# version 09
# speedup vs baseline: 1.1233x; 1.1233x over previous
from collections import defaultdict

import numpy as np
import ml_dtypes

import concourse.bass as bass
import concourse.mybir as mybir
from concourse.bass_utils import run_bass_kernel_spmd
from concourse.aot_env import allow_ici_all_to_all

F32 = mybir.dt.float32
BF16 = mybir.dt.bfloat16
AF = mybir.ActivationFunctionType
ALU = mybir.AluOpType
AX = mybir.AxisListType.X

S = 8192
D = 2048
NCORE = 8
TOK = S // NCORE
NT = 16
EPS = 1e-5
EXP_SCALE = 0.125
EXP_BIAS = -12.0
MASK_NEG = -30000.0


def build(taps=(), phases=5, nt=NT, p5_parts='rdwy', skip=()):
    nc = bass.Bass("TRN2", target_bir_lowering=False)

    d_x = nc.dram_tensor("x", [S, D], F32, kind="ExternalInput")
    d_xo = nc.dram_tensor("x_own", [TOK, D], F32, kind="ExternalInput")
    d_wqkv = nc.dram_tensor("wqkv", [D, 768], F32, kind="ExternalInput")
    d_bqkv = nc.dram_tensor("bqkv", [128, 6], F32, kind="ExternalInput")
    d_wout = nc.dram_tensor("wout", [4, D, 512], F32, kind="ExternalInput")
    d_wgate = nc.dram_tensor("wgate", [D, 8], F32, kind="ExternalInput")
    d_bout = nc.dram_tensor("bout", [1, D], F32, kind="ExternalInput")
    d_wgu = nc.dram_tensor("wgu", [8, 8, 16, 128, 256], F32, kind="ExternalInput")
    d_wdn = nc.dram_tensor("wdn", [8, 4, 8, 128, 512], F32, kind="ExternalInput")
    d_wsg = nc.dram_tensor("wsg", [4, 8, 16, 128, 256], F32, kind="ExternalInput")
    d_wsd = nc.dram_tensor("wsd", [4, 4, 8, 128, 512], F32, kind="ExternalInput")
    d_cs1 = nc.dram_tensor("cs1", [128, S], BF16, kind="ExternalInput")
    d_cs2 = nc.dram_tensor("cs2", [128, S], BF16, kind="ExternalInput")
    d_masks = nc.dram_tensor("masks", [4, 128, 512], BF16, kind="ExternalInput")
    d_ident = nc.dram_tensor("identb", [128, 128], BF16, kind="ExternalInput")
    d_ident32 = nc.dram_tensor("ident32", [128, 128], F32, kind="ExternalInput")
    d_swap = nc.dram_tensor("swapm", [128, 128], BF16, kind="ExternalInput")
    d_sel2 = nc.dram_tensor("sel2", [2, 128], BF16, kind="ExternalInput")
    d_ones1 = nc.dram_tensor("ones1", [1, 128], F32, kind="ExternalInput")

    d_y = nc.dram_tensor("y", [TOK, D], F32, kind="ExternalOutput")

    d_a2a_in = nc.dram_tensor("a2a_in", [D, TOK], BF16, kind="Internal")
    d_a2a_out = nc.dram_tensor("a2a_out", [D, TOK], BF16, kind="Internal")
    d_rl_in = nc.dram_tensor("rl_in", [64, 512], BF16, kind="Internal")
    d_rl_out = nc.dram_tensor("rl_out", [64, 512], BF16, kind="Internal")

    tap_t = {}
    if "qkv" in taps:
        tap_t["QT0"] = nc.dram_tensor("tQT0", [128, S], F32, kind="ExternalOutput")
        tap_t["KT1"] = nc.dram_tensor("tKT1", [128, S], F32, kind="ExternalOutput")
        tap_t["V0"] = nc.dram_tensor("tV0", [128, 64 * 65], F32, kind="ExternalOutput")
        tap_t["QT1"] = nc.dram_tensor("tQT1", [128, S], F32, kind="ExternalOutput")
        tap_t["KT0"] = nc.dram_tensor("tKT0", [128, S], F32, kind="ExternalOutput")
        tap_t["V2"] = nc.dram_tensor("tV2", [128, 64 * 65], F32, kind="ExternalOutput")
    if "attn" in taps:
        tap_t["A0"] = nc.dram_tensor("tA0", [128, S], F32, kind="ExternalOutput")
        tap_t["A1"] = nc.dram_tensor("tA1", [128, S], F32, kind="ExternalOutput")
        tap_t["L"] = nc.dram_tensor("tL", [64, 512], F32, kind="ExternalOutput")
    if "rb" in taps:
        tap_t["RB"] = nc.dram_tensor("tRB", [128, 16 * 1024], F32, kind="ExternalOutput")
        tap_t["CT"] = nc.dram_tensor("tCT", [8, 1024], F32, kind="ExternalOutput")
    if "rdbg" in taps:
        tap_t["LG"] = nc.dram_tensor("tLG", [8, 1024], F32, kind="ExternalOutput")
        tap_t["ES"] = nc.dram_tensor("tES", [128, 64], F32, kind="ExternalOutput")
        tap_t["CA"] = nc.dram_tensor("tCA", [128, 64], F32, kind="ExternalOutput")
        tap_t["CT2"] = nc.dram_tensor("tCT2", [8, 1024], F32, kind="ExternalOutput")
    if "p1dbg" in taps:
        tap_t["XS"] = nc.dram_tensor("tXS", [128, 2048], F32, kind="ExternalOutput")
        tap_t["XN"] = nc.dram_tensor("tXN", [128, 2048], F32, kind="ExternalOutput")
        tap_t["ST"] = nc.dram_tensor("tST", [128, 16], F32, kind="ExternalOutput")
        tap_t["XT"] = nc.dram_tensor("tXT", [128, 16 * 512], F32, kind="ExternalOutput")

    A = nc.alloc_sbuf_tensor
    AT = nc.alloc_sbuf_tensor_at
    ident = A("identS", [128, 128], BF16)
    ident32 = A("ident32S", [128, 128], F32)
    swapm = A("swapS", [128, 128], BF16)
    sel2 = A("sel2S", [2, 128], BF16)
    ones1 = A("ones1S", [1, 128], F32)
    bias_qkv = A("biasqkvS", [128, 6], F32)
    wgate_sb = A("wgateS", [128, 16, 8], BF16)
    bout_b = A("boutbS", [128, 2048], BF16)
    e_sb = A("esbS", [128, 64], F32)
    comb_all = A("comballS", [128, 64], F32)
    combine_T = A("combTS", [8, 1024], F32)
    c_bcast = A("cbcastS", [128, 1024], BF16)
    stats = A("statsS", [128, 16], F32)
    ebias = A("ebiasS", [128, 1], F32)
    sums_a = A("sumsS", [128, 64], F32)
    ssqs_a = A("ssqsS", [128, 64], F32)
    means_a = A("meansS", [128, 64], F32)
    rstds_a = A("rstdsS", [128, 64], F32)

    stage_l = [A(f"stagelS{i}", [1, 512], F32) for i in range(2)]

    R_S = A("R_S", [128, 10240], mybir.dt.uint8)
    adr_s = nc.lookup_mloc(R_S).addr
    cs1_t = [AT(f"cs1_{i}", [128, 512], BF16, offset=adr_s + 1024 * i) for i in range(2)]
    cs2_t = [AT(f"cs2_{i}", [128, 512], BF16, offset=adr_s + 2048 + 1024 * i) for i in range(2)]
    swp_t = [AT(f"swp_{i}", [128, 512], BF16, offset=adr_s + 4096 + 1024 * i) for i in range(2)]
    vtmp_t = [AT(f"vtmp_{i}", [128, 512], BF16, offset=adr_s + 6144 + 1024 * i) for i in range(2)]
    rope1 = AT("rope1", [128, 512], BF16, offset=adr_s + 8192)
    rope2 = AT("rope2", [128, 512], BF16, offset=adr_s + 9216)
    masks_sb = AT("masksS", [128, 4, 512], BF16, offset=adr_s + 0)
    pstrip = [AT(f"pstrip_{i}", [128, 1024], BF16, offset=adr_s + 4096 + 2048 * i)
              for i in range(3)]
    sg_t = [AT(f"sgt{i}", [128, 1024], BF16, offset=adr_s + 2048 * i) for i in range(2)]
    comb_stage = AT("combstS", [1, 1024], F32, offset=adr_s + 4096)

    R_A = A("R_A", [128, 98816], mybir.dt.uint8)
    adr_a = nc.lookup_mloc(R_A).addr
    QT = [AT(f"QT{i}", [128, S], BF16, offset=adr_a + 16384 * i) for i in range(2)]
    KT = [AT(f"KT{i}", [128, S], BF16, offset=adr_a + 32768 + 16384 * i) for i in range(2)]
    V65 = [AT(f"V65_{h}", [128, 64 * 65], BF16, offset=adr_a + 65536 + 8320 * h)
           for h in range(4)]
    Y_sb = AT("Ysb", [128, 8, 2048], F32, offset=adr_a + 0)
    act_sb = AT("actsb", [128, 8, 1024], BF16, offset=adr_a + 65536)
    rl_sb = AT("rlsb", [2, 16, 1024], BF16, offset=adr_a + 65536)

    R_B = A("R_B", [128, 40960], mybir.dt.uint8)
    adr_b = nc.lookup_mloc(R_B).addr
    x_stage = [AT(f"xst{i}", [128, 2048], F32, offset=adr_b + 8192 * i) for i in range(2)]
    xn_t = [AT(f"xnt{i}", [128, 2048], BF16, offset=adr_b + 16384 + 4096 * i)
            for i in range(2)]
    xnT_cur = AT("xnTcur", [128, 16, 512], BF16, offset=adr_b + 24576)
    A_t = [AT(f"At{i}", [128, 8, 1024], BF16, offset=adr_b + 16384 * i) for i in range(2)]
    xn_ownT = AT("xnownT", [128, 16, 1024], BF16, offset=adr_b + 0)
    AT_rb = AT("ATrb", [128, 16, 1024], BF16, offset=adr_b + 0)

    R_C = A("R_C", [128, 32768], mybir.dt.uint8)
    adr_c = nc.lookup_mloc(R_C).addr
    wqkv_sb = AT("wqkvS", [128, 16, 768], BF16, offset=adr_c)
    l_buf = AT("lbufS", [64, 512], F32, offset=adr_c + 0)
    rl_buf = AT("rlbufS", [64, 512], BF16, offset=adr_c + 2048)
    logits_sb = AT("logitsS", [8, 1024], F32, offset=adr_c + 3072)
    xo_stage = [AT(f"xost{i}", [128, 2048], F32, offset=adr_c + 8192 * i) for i in range(2)]
    xn_to = [AT(f"xnto{i}", [128, 2048], BF16, offset=adr_c + 16384 + 4096 * i)
             for i in range(2)]
    bout_sb = AT("boutS", [1, 2048], F32, offset=adr_c + 24576)
    x_stage2 = [AT("xst2_0", [128, 2048], F32, offset=adr_c + 24576),
                A("xst2_1", [128, 2048], F32)]
    sgw = [AT(f"sgw{i}", [128, 16, 256], BF16, offset=adr_c + 8192 * i) for i in range(2)]
    sdw = [AT(f"sdw{i}", [128, 8, 512], BF16, offset=adr_c + 16384 + 8192 * i)
           for i in range(2)]
    wob = [AT(f"wob{i}", [128, 16, 512], BF16, offset=adr_c + 16384 * i) for i in range(2)]

    PA = nc.alloc_psum_tensor("PA", [128, 1024], F32)
    PB = nc.alloc_psum_tensor("PB", [128, 1024], F32)
    PC = nc.alloc_psum_tensor("PC", [128, 1024], F32)
    b6 = nc.psum_base
    PD32 = nc.alloc_psum_tensor("PD32", [128, 1024], F32)
    nc.psum_base = b6
    PDT = nc.alloc_psum_tensor("PDT", [128, 2048], BF16)
    qk_banks = [PA[:, 0:512], PA[:, 512:1024], PB[:, 0:512], PB[:, 512:1024]]

    sems = {}
    for n in ("dx0", "dx1", "dcs0", "dcs1", "dwq", "dmisc", "dmsk", "dcb",
              "dxo0", "dxo1", "dgu0", "dgu1", "ddn0", "ddn1",
              "dst", "drb", "dyo", "dtap",
              "c_sq", "c_vr", "c_rs", "c_ln", "c_tp", "c_te", "c_qk", "c_qe",
              "c_sw", "c_se", "c_ro", "c_vt", "c_ve",
              "c_ss", "c_mk", "c_ex", "c_pv", "c_oe", "c_rl", "c_cc",
              "c_rg", "c_rte", "c_rex", "c_rtr", "c_rv", "c_rt2", "c_rce",
              "c_bc", "c_be", "c_gu", "c_si", "c_am", "c_dn", "c_ya",
              "c_dv", "c_wo", "c_vq", "c_lx", "dlx", "bar"):
        sems[n] = nc.alloc_semaphore(n)

    led = defaultdict(int)
    plan = {k: [] for k in "TVAYG"}

    def load_act_set(e, idx):
        return e.add_instruction(mybir.InstLoadActFuncSet(
            name=nc.get_next_instruction_name(), act_func_set_id=idx,
            ins=[], outs=[]))

    def run(eng, fn):
        plan[eng].append(fn)

    def wait(eng, name, val):
        if val > 0:
            plan[eng].append(lambda e, s=sems[name], v=val: e.wait_ge(s, v))

    def step(eng, fn, inc=None, incn=1):
        if inc is None:
            plan[eng].append(fn)
        else:
            led[inc] += incn
            plan[eng].append(
                lambda e, f=fn, ss=sems[inc], n=incn: f(e).then_inc(ss, n))

    def vstep(fn):
        step("V", fn, "c_vq")
        wait("V", "c_vq", led["c_vq"])

    def barrier():
        tgt = led["bar"] + 5
        led["bar"] += 5
        for k in "TVAYG":
            plan[k].append(lambda e, ss=sems["bar"]: e.nop().then_inc(ss, 1))
        for k in "TVAYG":
            plan[k].append(lambda e, s=sems["bar"], v=tgt: e.wait_ge(s, v))

    def p0():
        step("Y", lambda e: e.dma_start(out=ident[:, :], in_=d_ident.ap()),
             "dmisc", 16)
        step("Y", lambda e: e.dma_start(out=ident32[:, :], in_=d_ident32.ap()),
             "dmisc", 16)
        step("Y", lambda e: e.dma_start(out=swapm[:, :], in_=d_swap.ap()),
             "dmisc", 16)
        step("Y", lambda e: e.dma_start(out=sel2[:, :], in_=d_sel2.ap()),
             "dmisc", 16)
        step("Y", lambda e: e.dma_start(out=ones1[:, :], in_=d_ones1.ap()),
             "dmisc", 16)
        step("Y", lambda e: e.dma_start(out=bias_qkv[:, :], in_=d_bqkv.ap()),
             "dmisc", 16)
        step("G", lambda e: e.dma_start(
            out=wgate_sb[:, :, :],
            in_=d_wgate.ap().rearrange("(f p) c -> p f c", p=128)), "dmisc", 16)
        step("G", lambda e: e.dma_start(
            out=wqkv_sb[:, :, :],
            in_=d_wqkv.ap().rearrange("(f p) c -> p f c", p=128)), "dwq", 16)
        step("Y", lambda e: e.dma_start(out=cs1_t[0][:, :], in_=d_cs1.ap()[:, 0:512]),
             "dcs0", 16)
        step("Y", lambda e: e.dma_start(out=cs2_t[0][:, :], in_=d_cs2.ap()[:, 0:512]),
             "dcs0", 16)
        for h in range(4):
            step("V", lambda e, h=h: e.memset(V65[h][:, :], 1.0), "c_ve")
        step("V", lambda e: e.memset(ebias[:, :], EXP_BIAS), "c_rl")
        run("A", lambda e: load_act_set(e, 6))

    def p1():
        wait("T", "dmisc", led["dmisc"])
        wait("A", "dmisc", led["dmisc"])
        pdt_evac = [0, 0]
        qkb_uses = [0] * 4
        swp_uses = [0, 0]
        swp_rope = [0, 0]
        cs_ro = [0, 0]
        for T in range(nt):
            stg = [x_stage[0], x_stage[1], x_stage2[0], x_stage2[1]]
            for st in range(4):
                g = 4 * T + st
                par = g % 4
                if g >= 4:
                    wait("Y", "c_sq", g - 3)
                    wait("Y", "c_ln", g - 3)
                step("Y", lambda e, g=g, par=par: e.dma_start(
                    out=stg[par][:, :], in_=d_x.ap()[128 * g:128 * (g + 1), :]),
                    "dx%d" % (par % 2), 16)
                wait("A", "dx%d" % (par % 2), led["dx%d" % (par % 2)])
                step("A", lambda e, g=g, par=par: e.activation(
                    bout_b[:, :], stg[par][:, :], AF.Square,
                    accum_out=ssqs_a[:, g:g + 1]), "c_sq")
                wait("V", "dx%d" % (par % 2), led["dx%d" % (par % 2)])
                step("V", lambda e, g=g, par=par: e.reduce_sum(
                    sums_a[:, g:g + 1], stg[par][:, :], axis=AX), "c_vq")
            wait("V", "c_vq", led["c_vq"])
            wait("V", "c_sq", 4 * (T + 1))
            sl = slice(4 * T, 4 * T + 4)
            vstep(lambda e, sl=sl: e.tensor_scalar(
                means_a[:, sl], sums_a[:, sl], 1.0 / D, None, op0=ALU.mult))
            vstep(lambda e, sl=sl: e.tensor_scalar(
                rstds_a[:, sl], ssqs_a[:, sl], 1.0 / D, None, op0=ALU.mult))
            vstep(lambda e, sl=sl: e.tensor_mul(
                stats[:, 0:4], means_a[:, sl], means_a[:, sl]))
            vstep(lambda e, sl=sl: e.tensor_sub(
                rstds_a[:, sl], rstds_a[:, sl], stats[:, 0:4]))
            step("V", lambda e, sl=sl: e.tensor_scalar(
                rstds_a[:, sl], rstds_a[:, sl], EPS, None, op0=ALU.add), "c_vr")
            wait("A", "c_vr", led["c_vr"])

            def act_rstd(e, sl=sl):
                e.activation(stats[:, 4:8], rstds_a[:, sl], AF.Ln)
                return e.activation(rstds_a[:, sl], stats[:, 4:8], AF.Exp,
                                    scale=-0.5)
            step("A", act_rstd, "c_rs")
            for st in range(4):
                g = 4 * T + st
                par = g % 4
                wait("V", "c_rs", led["c_rs"])
                if g >= 2:
                    wait("V", "c_tp", 2 * (g - 1))
                step("V", lambda e, g=g, par=par: e.tensor_scalar(
                    xn_t[g % 2][:, :], stg[par][:, :],
                    means_a[:, g:g + 1], rstds_a[:, g:g + 1],
                    op0=ALU.subtract, op1=ALU.mult), "c_ln")
                for half in range(2):
                    wait("T", "c_ln", led["c_ln"])
                    wait("T", "c_te", pdt_evac[half])
                    wait("T", "c_ve", 4 + T)

                    def trs(e, g=g, half=half):
                        last = None
                        for j in range(8):
                            f = 8 * half + j
                            last = e.transpose(
                                PDT[:, 1024 * half + 128 * j:
                                    1024 * half + 128 * (j + 1)],
                                xn_t[g % 2][:, 128 * f:128 * (f + 1)], ident[:, :])
                        return last
                    step("T", trs, "c_tp")
                    wait("V", "c_tp", led["c_tp"])
                    step("V", lambda e, st=st, half=half: e.tensor_copy(
                        xnT_cur[:, 8 * half:8 * (half + 1), 128 * st:128 * (st + 1)],
                        PDT[:, 1024 * half:1024 * (half + 1)].rearrange(
                            "p (j c) -> p j c", j=8)), "c_te")
                    pdt_evac[half] = led["c_te"]
            if T == 0:
                wait("T", "dwq", 16)
            for cc in range(6):
                bank = cc % 4
                wait("T", "c_te", led["c_te"])
                wait("T", "c_qe", qkb_uses[bank])

                def qkv_mm(e, cc=cc, bank=bank):
                    last = None
                    for f in range(16):
                        last = e.matmul(qk_banks[bank],
                                        wqkv_sb[:, f, 128 * cc:128 * (cc + 1)],
                                        xnT_cur[:, f, :],
                                        start=(f == 0), stop=(f == 15))
                    return last
                step("T", qkv_mm, "c_qk")
                wait("A", "c_qk", led["c_qk"])
                if cc >= 4 and T > 0:
                    wait("A", "c_vt", led["c_vt"])

                def qkv_ev(e, T=T, cc=cc, bank=bank):
                    if cc < 2:
                        dst = QT[cc][:, 512 * T:512 * (T + 1)]
                    elif cc < 4:
                        dst = KT[cc - 2][:, 512 * T:512 * (T + 1)]
                    else:
                        dst = vtmp_t[cc - 4][:, :]
                    return e.activation(dst, qk_banks[bank], AF.Identity,
                                        bias=bias_qkv[:, cc:cc + 1])
                step("A", qkv_ev, "c_qe")
                qkb_uses[bank] = led["c_qe"]
            for cc in range(4):
                tile = QT[cc] if cc < 2 else KT[cc - 2]
                half = cc % 2
                wait("T", "c_qe", 6 * T + cc + 1)
                wait("T", "c_se", swp_uses[half])
                step("T", lambda e, tile=tile, T=T, half=half: e.matmul(
                    PC[:, 512 * half:512 * (half + 1)], swapm[:, :],
                    tile[:, 512 * T:512 * (T + 1)], start=True, stop=True), "c_sw")
                wait("A", "c_sw", led["c_sw"])
                wait("A", "c_ro", swp_rope[half])
                step("A", lambda e, half=half: e.activation(
                    swp_t[half][:, :], PC[:, 512 * half:512 * (half + 1)], AF.Copy),
                    "c_se")
                swp_uses[half] = led["c_se"]
                wait("V", "c_se", led["c_se"])
                wait("V", "dcs%d" % (T % 2), led["dcs%d" % (T % 2)])

                def rope(e, tile=tile, T=T, half=half, par=T % 2):
                    sl = tile[:, 512 * T:512 * (T + 1)]
                    e.tensor_mul(rope1[:, :], sl, cs1_t[par][:, :])
                    e.tensor_mul(rope2[:, :], swp_t[half][:, :], cs2_t[par][:, :])
                    return e.tensor_add(sl, rope1[:, :], rope2[:, :])
                step("V", rope, "c_ro")
                swp_rope[half] = led["c_ro"]
            cs_ro[T % 2] = led["c_ro"]
            if T + 1 < nt:
                par = (T + 1) % 2
                wait("Y", "c_ro", cs_ro[par])
                step("Y", lambda e, T=T: e.dma_start(
                    out=cs1_t[(T + 1) % 2][:, :],
                    in_=d_cs1.ap()[:, 512 * (T + 1):512 * (T + 2)]), "dcs%d" % par, 16)
                step("Y", lambda e, T=T: e.dma_start(
                    out=cs2_t[(T + 1) % 2][:, :],
                    in_=d_cs2.ap()[:, 512 * (T + 1):512 * (T + 2)]), "dcs%d" % par, 16)
            for vv in range(2):
                wait("T", "c_qe", 6 * T + 5 + vv)
                wait("T", "c_te", led["c_te"])
                if T > 0:
                    wait("T", "c_ve", 4 + T)

                def vtr(e, vv=vv):
                    last = None
                    for j in range(4):
                        last = e.transpose(
                            PDT[:, 1024 * vv + 128 * j:1024 * vv + 128 * (j + 1)],
                            vtmp_t[vv][:, 128 * j:128 * (j + 1)], ident[:, :])
                    return last
                step("T", vtr, "c_vt")
            wait("V", "c_vt", led["c_vt"])

            def vev(e, T=T):
                last = None
                for vv in range(2):
                    for j in range(4):
                        tc = 4 * T + j
                        for a in range(2):
                            last = e.tensor_copy(
                                V65[2 * vv + a][:, 65 * tc:65 * tc + 64],
                                PDT[:, 1024 * vv + 128 * j + 64 * a:
                                    1024 * vv + 128 * j + 64 * (a + 1)])
                return last
            step("V", vev, "c_ve")
        if "XS" in tap_t:
            run("G", lambda e: e.wait_ge(sems["c_ve"], led["c_ve"]))
            run("G", lambda e: e.wait_ge(sems["c_ro"], led["c_ro"]))
            step("G", lambda e: e.dma_start(out=tap_t["XS"].ap(), in_=x_stage[1][:, :]),
                 "dtap", 16)
            step("G", lambda e: e.dma_start(out=tap_t["XN"].ap(), in_=xn_t[1][:, :]),
                 "dtap", 16)
            step("G", lambda e: e.dma_start(out=tap_t["ST"].ap(), in_=stats[:, :]),
                 "dtap", 16)
            step("G", lambda e: e.dma_start(
                out=tap_t["XT"].ap(),
                in_=xnT_cur[:, :, :].rearrange("p f c -> p (f c)")), "dtap", 16)
        if "QT0" in tap_t:
            run("G", lambda e: e.wait_ge(sems["c_ro"], led["c_ro"]))
            run("G", lambda e: e.wait_ge(sems["c_ve"], led["c_ve"]))
            step("G", lambda e: e.dma_start(out=tap_t["QT0"].ap(), in_=QT[0][:, :]),
                 "dtap", 16)
            step("G", lambda e: e.dma_start(out=tap_t["KT1"].ap(), in_=KT[1][:, :]),
                 "dtap", 16)
            step("G", lambda e: e.dma_start(out=tap_t["V0"].ap(), in_=V65[0][:, :]),
                 "dtap", 16)
            step("G", lambda e: e.dma_start(out=tap_t["QT1"].ap(), in_=QT[1][:, :]),
                 "dtap", 16)
            step("G", lambda e: e.dma_start(out=tap_t["KT0"].ap(), in_=KT[0][:, :]),
                 "dtap", 16)
            step("G", lambda e: e.dma_start(out=tap_t["V2"].ap(), in_=V65[2][:, :]),
                 "dtap", 16)

    def p2():
        step("Y", lambda e: e.dma_start(
            out=masks_sb[:, :, :],
            in_=d_masks.ap().rearrange("v p c -> p v c")), "dmsk", 16)
        wait("A", "c_rl", 1)
        sb_map = [PA, PB, PD32]
        sb_last_exp = [0, 0, 0]
        strip_last_pv = [0, 0, 0]
        o_last_evac = [0, 0]
        units = []
        for h in range(4):
            for q in range(NT):
                npair = 2 * (q + 1)
                for p in range(npair):
                    units.append((h, q, p, npair))
        DEPTH = 3

        def emit_pv(u):
            h, q, p, npair = units[u]
            v = V65[h]
            ooff = 512 * ((h * NT + q) % 2)
            strip = pstrip[u % 3]
            wait("T", "c_ex", u + 1)

            def pv(e, v=v, p=p, strip=strip, ooff=ooff,
                   first=(p == 0), last=(p == npair - 1)):
                e.matmul(PC[0:65, ooff:ooff + 512],
                         v[:, 65 * 2 * p:65 * 2 * p + 65],
                         strip[:, 0:512], start=first, stop=False)
                return e.matmul(PC[0:65, ooff:ooff + 512],
                                v[:, 65 * (2 * p + 1):65 * (2 * p + 1) + 65],
                                strip[:, 512:1024], start=False, stop=last)
            step("T", pv, "c_pv")
            strip_last_pv[u % 3] = led["c_pv"]
            if p == npair - 1:
                wait("V", "c_pv", led["c_pv"])
                lpar = (h * NT + q) % 2
                if led["c_lx"] >= 2:
                    wait("V", "dlx", 16 * (led["c_lx"] - 1))
                r0 = 64 * (h % 2)
                step("V", lambda e, ooff=ooff, lpar=lpar: e.tensor_copy(
                    stage_l[lpar][:, :], PC[64:65, ooff:ooff + 512]), "c_lx")
                step("V", lambda e, h=h, q=q, r0=r0, ooff=ooff: e.tensor_copy(
                    A_t[h // 2][r0:r0 + 64, q // 2,
                                512 * (q % 2):512 * (q % 2) + 512],
                    PC[0:64, ooff:ooff + 512]), "c_oe")
                wait("Y", "c_lx", led["c_lx"])
                step("Y", lambda e, h=h, q=q, lpar=lpar: e.dma_start(
                    out=l_buf[16 * h + q:16 * h + q + 1, :],
                    in_=stage_l[lpar][:, :]), "dlx", 16)
                o_last_evac[(h * NT + q) % 2] = led["c_oe"]

        for u, (h, q, p, npair) in enumerate(units):
            qt, kt = QT[h // 2], KT[h // 2]
            r0 = 64 * (h % 2)
            sb = sb_map[u % 3]
            strip = pstrip[u % 3]
            if p == 0:
                wait("T", "c_oe", o_last_evac[(h * NT + q) % 2])
            wait("T", "c_ex", sb_last_exp[u % 3])

            masked = (p >= 2 * q)
            if masked and u < 3:
                wait("T", "dmsk", 16)

            def smm2(e, sb=sb, kt=kt, qt=qt, r0=r0, p=p, q=q, masked=masked):
                voff = 2 * (p - 2 * q)
                e.matmul(sb[:, 0:512],
                         kt[r0:r0 + 64, 256 * p:256 * p + 128],
                         qt[r0:r0 + 64, 512 * q:512 * (q + 1)],
                         start=True, stop=not masked)
                if masked:
                    e.matmul(sb[:, 0:512], ident[:, :], masks_sb[:, voff, :],
                             start=False, stop=True)
                last = e.matmul(sb[:, 512:1024],
                                kt[r0:r0 + 64, 256 * p + 128:256 * (p + 1)],
                                qt[r0:r0 + 64, 512 * q:512 * (q + 1)],
                                start=True, stop=not masked)
                if masked:
                    last = e.matmul(sb[:, 512:1024], ident[:, :],
                                    masks_sb[:, voff + 1, :],
                                    start=False, stop=True)
                return last
            step("T", smm2, "c_ss")
            if u >= DEPTH:
                emit_pv(u - DEPTH)
            wait("A", "c_ss", led["c_ss"])
            wait("A", "c_pv", strip_last_pv[u % 3])
            step("A", lambda e, strip=strip, sb=sb: e.activation(
                strip[:, :], sb[:, :], AF.Exp,
                bias=ebias[:, 0:1], scale=EXP_SCALE), "c_ex")
            sb_last_exp[u % 3] = led["c_ex"]
        for u in range(len(units) - DEPTH, len(units)):
            emit_pv(u)
        wait("V", "c_oe", led["c_oe"])
        wait("V", "dlx", led["dlx"])
        vstep(lambda e: e.reciprocal(l_buf[:, :], l_buf[:, :]))
        step("V", lambda e: e.tensor_copy(rl_buf[:, :], l_buf[:, :]), "c_rl")
        if "A0" in tap_t:
            run("G", lambda e: e.wait_ge(sems["c_rl"], led["c_rl"]))
            step("G", lambda e: e.dma_start(
                out=tap_t["A0"].ap(),
                in_=A_t[0][:, :, :].rearrange("p j c -> p (j c)")), "dtap", 16)
            step("G", lambda e: e.dma_start(
                out=tap_t["A1"].ap(),
                in_=A_t[1][:, :, :].rearrange("p j c -> p (j c)")), "dtap", 16)
            step("G", lambda e: e.dma_start(out=tap_t["L"].ap(), in_=l_buf[:, :]),
                 "dtap", 16)

    def p25():
        for tt in range(2):
            step("Y", lambda e, tt=tt: e.dma_start(
                out=d_a2a_in.ap().rearrange("(j t p) c -> t p j c",
                                            j=8, t=2, p=128)[tt],
                in_=A_t[tt][:, :, :]), "dst", 16)
        for lh in range(4):
            step("Y", lambda e, lh=lh: e.dma_start(
                out=d_rl_in.ap().rearrange("(j lh jj) c -> lh j jj c",
                                           j=8, lh=4, jj=2)[lh],
                in_=rl_buf[16 * lh:16 * (lh + 1), :]), "dst", 16)
        wait("G", "dst", led["dst"])

        def cc1(e):
            with allow_ici_all_to_all():
                return e.collective_compute(
                    "AllToAll", ALU.bypass, replica_groups=[list(range(NCORE))],
                    ins=[d_a2a_in.ap().opt()], outs=[d_a2a_out.ap().opt()])
        step("G", cc1, "c_cc")

        def cc2(e):
            with allow_ici_all_to_all():
                return e.collective_compute(
                    "AllToAll", ALU.bypass, replica_groups=[list(range(NCORE))],
                    ins=[d_rl_in.ap().opt()], outs=[d_rl_out.ap().opt()])
        step("G", cc2, "c_cc")

    def p3_xn_own():
        for g in range(8):
            par = g % 2
            if g >= 2:
                wait("Y", "c_sq", led["c_sq"] - 1)
                wait("Y", "c_ln", led["c_ln"] - 1)
            step("Y", lambda e, g=g, par=par: e.dma_start(
                out=xo_stage[par][:, :], in_=d_xo.ap()[128 * g:128 * (g + 1), :]),
                "dxo%d" % par, 16)
            step("Y", lambda e, g=g, par=par: e.dma_start(
                out=Y_sb[:, g, :], in_=d_xo.ap()[128 * g:128 * (g + 1), :]),
                "dxo%d" % par, 16)
            wait("A", "dxo%d" % par, led["dxo%d" % par])
            if g >= 2:
                wait("A", "c_ln", led["c_ln"] - 1)
                wait("A", "c_tp", led["c_tp"] - 2)
            step("A", lambda e, par=par: e.activation(
                xn_to[par][:, :], xo_stage[par][:, :], AF.Square,
                accum_out=stats[:, 4 + par:5 + par]), "c_sq")
            wait("V", "dxo%d" % par, led["dxo%d" % par])
            vstep(lambda e, par=par: e.reduce_sum(
                stats[:, 0 + par:1 + par], xo_stage[par][:, :], axis=AX))
            wait("V", "c_sq", led["c_sq"])
            vstep(lambda e, par=par: e.tensor_scalar(
                stats[:, 6 + par:7 + par], stats[:, 0 + par:1 + par],
                1.0 / D, None, op0=ALU.mult))
            vstep(lambda e, par=par: e.tensor_scalar(
                stats[:, 8 + par:9 + par], stats[:, 4 + par:5 + par],
                1.0 / D, None, op0=ALU.mult))
            vstep(lambda e, par=par: e.tensor_mul(
                stats[:, 10 + par:11 + par], stats[:, 6 + par:7 + par],
                stats[:, 6 + par:7 + par]))
            vstep(lambda e, par=par: e.tensor_sub(
                stats[:, 8 + par:9 + par], stats[:, 8 + par:9 + par],
                stats[:, 10 + par:11 + par]))
            step("V", lambda e, par=par: e.tensor_scalar(
                stats[:, 8 + par:9 + par], stats[:, 8 + par:9 + par],
                EPS, None, op0=ALU.add), "c_vr")
            wait("A", "c_vr", led["c_vr"])

            def act_rstd(e, par=par):
                e.activation(stats[:, 14:15], stats[:, 8 + par:9 + par], AF.Ln)
                return e.activation(stats[:, 12 + par:13 + par], stats[:, 14:15],
                                    AF.Exp, scale=-0.5)
            step("A", act_rstd, "c_rs")
            wait("V", "c_rs", led["c_rs"])
            step("V", lambda e, par=par: e.tensor_scalar(
                xn_to[par][:, :], xo_stage[par][:, :],
                stats[:, 6 + par:7 + par], stats[:, 12 + par:13 + par],
                op0=ALU.subtract, op1=ALU.mult), "c_ln")
            for half in range(2):
                wait("T", "c_ln", led["c_ln"])
                if g > 0:
                    wait("T", "c_te", led["c_te"] - 1)

                def trs(e, par=par, half=half):
                    last = None
                    for j in range(8):
                        f = 8 * half + j
                        last = e.transpose(
                            PDT[:, 1024 * half + 128 * j:1024 * half + 128 * (j + 1)],
                            xn_to[par][:, 128 * f:128 * (f + 1)], ident[:, :])
                    return last
                step("T", trs, "c_tp")
                wait("V", "c_tp", led["c_tp"])
                if g == 0 and half == 0:
                    wait("V", "dst", led["dst"])
                step("V", lambda e, g=g, half=half: e.tensor_copy(
                    xn_ownT[:, 8 * half:8 * (half + 1), 128 * g:128 * (g + 1)],
                    PDT[:, 1024 * half:1024 * (half + 1)].rearrange(
                        "p (j c) -> p j c", j=8)), "c_te")

    def p3_router():
        step("Y", lambda e: e.dma_start(out=bout_sb[:, :], in_=d_bout.ap()),
             "dmisc", 16)
        wait("T", "dmisc", led["dmisc"])
        wait("T", "c_ex", led["c_ex"])

        def bb(e):
            last = None
            tgts = [PA[:, 0:512], PA[:, 512:1024], PB[:, 0:512], PB[:, 512:1024]]
            for j in range(4):
                last = e.matmul(tgts[j], ones1[:, :],
                                bout_sb[:, 512 * j:512 * (j + 1)],
                                start=True, stop=True)
            return last
        step("T", bb, "c_bc")
        wait("V", "c_bc", led["c_bc"])

        def bbev(e):
            e.tensor_copy(bout_b[:, 0:1024], PA[:, :])
            return e.tensor_copy(bout_b[:, 1024:2048], PB[:, :])
        step("V", bbev, "c_be")
        wait("T", "c_te", led["c_te"])
        wait("T", "c_be", led["c_be"])
        for th in range(2):
            def lg(e, th=th):
                last = None
                for f in range(16):
                    last = e.matmul(PA[0:8, 512 * th:512 * (th + 1)],
                                    wgate_sb[:, f, :],
                                    xn_ownT[:, f, 512 * th:512 * (th + 1)],
                                    start=(f == 0), stop=(f == 15))
                return last
            step("T", lg, "c_rg")
        wait("A", "c_rg", led["c_rg"])
        step("A", lambda e: e.activation(logits_sb[:, :], PA[0:8, :], AF.Copy),
             "c_rte")
        wait("T", "c_rte", led["c_rte"])

        def ltr(e):
            last = None
            for j in range(8):
                last = e.transpose(PB[:, 8 * j:8 * (j + 1)],
                                   logits_sb[:, 128 * j:128 * (j + 1)],
                                   ident32[0:8, 0:8])
            return last
        step("T", ltr, "c_rtr")
        wait("A", "c_rtr", led["c_rtr"])
        step("A", lambda e: e.activation(e_sb[:, :], PB[:, 0:64], AF.Exp), "c_rex")
        wait("V", "c_rex", led["c_rex"])
        for j in range(8):
            es = e_sb[:, 8 * j:8 * (j + 1)]
            ca = comb_all[:, 8 * j:8 * (j + 1)]
            m1, m2, t_ = stats[:, 0:1], stats[:, 1:2], stats[:, 2:3]
            vstep(lambda e, es=es, m1=m1: e.reduce_max(m1, es, axis=AX))
            vstep(lambda e, es=es, ca=ca, m1=m1: e.tensor_scalar(
                ca, es, m1, None, op0=ALU.is_ge))
            tmp8 = stats[:, 8:16]
            vstep(lambda e, es=es, ca=ca, tmp8=tmp8: e.tensor_mul(tmp8, ca, es))
            vstep(lambda e, es=es, ca=ca, tmp8=tmp8: e.tensor_sub(ca, es, tmp8))
            vstep(lambda e, ca=ca, m2=m2: e.reduce_max(m2, ca, axis=AX))
            vstep(lambda e, es=es, ca=ca, m2=m2: e.tensor_scalar(
                ca, es, m2, None, op0=ALU.is_ge))
            vstep(lambda e, es=es, ca=ca: e.tensor_mul(ca, ca, es))
            vstep(lambda e, m1=m1, m2=m2, t_=t_: e.tensor_add(t_, m1, m2))
            vstep(lambda e, t_=t_: e.reciprocal(t_, t_))
            vstep(lambda e, ca=ca, t_=t_: e.tensor_scalar(
                ca, ca, t_, None, op0=ALU.mult))
        step("V", lambda e: e.engine_nop(), "c_rv")
        wait("T", "c_rv", led["c_rv"])

        def ctr(e):
            last = None
            for j in range(8):
                last = e.transpose(PC[0:8, 128 * j:128 * (j + 1)],
                                   comb_all[:, 8 * j:8 * (j + 1)], ident32[:, :])
            return last
        step("T", ctr, "c_rt2")
        wait("V", "c_rt2", led["c_rt2"])
        step("V", lambda e: e.tensor_copy(combine_T[:, :], PC[0:8, :]), "c_rce")
        wait("V", "dxo0", led["dxo0"])
        wait("V", "dxo1", led["dxo1"])

        def yinit(e):
            last = None
            for c in range(8):
                last = e.tensor_add(Y_sb[:, c, :], Y_sb[:, c, :], bout_b[:, :])
            return last
        step("V", yinit, "c_ya")
        if "LG" in tap_t:
            run("G", lambda e: e.wait_ge(sems["c_rce"], led["c_rce"]))
            step("G", lambda e: e.dma_start(out=tap_t["LG"].ap(), in_=logits_sb[:, :]),
                 "dtap", 16)
            step("G", lambda e: e.dma_start(out=tap_t["ES"].ap(), in_=e_sb[:, :]),
                 "dtap", 16)
            step("G", lambda e: e.dma_start(out=tap_t["CA"].ap(), in_=comb_all[:, :]),
                 "dtap", 16)
            step("G", lambda e: e.dma_start(out=tap_t["CT2"].ap(), in_=combine_T[:, :]),
                 "dtap", 16)
        run("A", lambda e: load_act_set(e, 18))
        if "CT" in tap_t:
            run("G", lambda e: e.wait_ge(sems["c_rce"], led["c_rce"]))
            step("G", lambda e: e.dma_start(out=tap_t["CT"].ap(),
                                            in_=combine_T[:, :]), "dtap", 16)

    gu_chunk = [0]
    dn_chunk = [0]
    gu_dma = [0, 0]
    dn_dma = [0, 0]
    gu_targets = []
    dn_targets = []

    def ffn_weights(kind, e_idx, first):
        for i in range(8):
            n = gu_chunk[0]
            par = n % 2
            if first and i < 2:
                wait("G", "c_te", led["c_te"])
                wait("G", "c_rce", led["c_rce"])
                wait("G", "dst", led["dst"])
            if n >= 2:
                wait("G", "c_gu", n - 1)
            src = (d_wsg.ap()[e_idx, i] if kind == "sh" else d_wgu.ap()[e_idx, i])
            step("G", lambda e, src=src, par=par: e.dma_start(
                out=sgw[par][:, :, :],
                in_=src.rearrange("f p c -> p f c")), "dgu%d" % par, 16)
            gu_dma[par] += 1
            gu_targets.append(16 * gu_dma[par])
            gu_chunk[0] += 1
        for hc in range(4):
            m = dn_chunk[0]
            par = m % 2
            if first and hc < 2:
                wait("G", "c_te", led["c_te"])
                if par == 1:
                    wait("G", "c_bc", 1)
            if m >= 2:
                wait("G", "c_dn", 8 * (m - 1))
            src = (d_wsd.ap()[e_idx, hc] if kind == "sh" else d_wdn.ap()[e_idx, hc])
            step("G", lambda e, src=src, par=par: e.dma_start(
                out=sdw[par][:, :, :],
                in_=src.rearrange("i p c -> p i c")), "ddn%d" % par, 16)
            dn_dma[par] += 1
            dn_targets.append(16 * dn_dma[par])
            dn_chunk[0] += 1

    gu_use = [0]
    dn_use = [0]

    def ffn_block(kind, e_idx):
        ya_start = led["c_ya"]
        if kind == "ex":
            wait("Y", "c_rce", led["c_rce"])
            wait("Y", "c_bc", led["c_bc"])
            step("Y", lambda e, ex=e_idx: e.dma_start(
                out=comb_stage[:, :], in_=combine_T[ex:ex + 1, :]), "dcb", 16)
            wait("T", "dcb", led["dcb"])
            wait("T", "c_am", led["c_am"])

            def cb(e):
                e.matmul(PD32[:, 0:512], ones1[:, :], comb_stage[0:1, 0:512],
                         start=True, stop=True)
                return e.matmul(PD32[:, 512:1024], ones1[:, :],
                                comb_stage[0:1, 512:1024],
                                start=True, stop=True)
            step("T", cb, "c_bc")
            wait("V", "c_bc", led["c_bc"])
            step("V", lambda e: e.tensor_copy(c_bcast[:, :], PD32[:, :]), "c_be")
        for i in range(8):
            n = gu_use[0]
            par = n % 2
            wait("T", "dgu%d" % par, gu_targets[n])
            gu_use[0] += 1
            gb, ub = (PA, PB) if par == 0 else (PC, PD32)
            if i == 0:
                wait("T", "c_ya", ya_start)
                wait("T", "c_rt2", led["c_rt2"])
            wait("T", "c_am", max(0, led["c_am"] - 1))
            if kind == "ex":
                wait("T", "c_be", led["c_be"])

            def gumm(e, par=par, gb=gb, ub=ub):
                last = None
                for f in range(16):
                    for th in range(2):
                        e.matmul(gb[:, 512 * th:512 * (th + 1)],
                                 sgw[par][:, f, 0:128],
                                 xn_ownT[:, f, 512 * th:512 * (th + 1)],
                                 start=(f == 0), stop=(f == 15))
                        last = e.matmul(ub[:, 512 * th:512 * (th + 1)],
                                        sgw[par][:, f, 128:256],
                                        xn_ownT[:, f, 512 * th:512 * (th + 1)],
                                        start=(f == 0), stop=(f == 15))
                return last
            step("T", gumm, "c_gu")
            wait("A", "c_gu", led["c_gu"])
            if led["c_am"] >= 2:
                wait("A", "c_am", led["c_am"] - 1)
            step("A", lambda e, gb=gb, par=par: e.activation(
                sg_t[par][:, :], gb[:, :], AF.Silu), "c_si")
            wait("V", "c_si", led["c_si"])
            if kind == "ex":
                vstep(lambda e, ub=ub, par=par, i=i: e.tensor_mul(
                    act_sb[:, i, :], sg_t[par][:, :], ub[:, :]))
                step("V", lambda e, i=i: e.tensor_mul(
                    act_sb[:, i, :], act_sb[:, i, :], c_bcast[:, :]), "c_am")
            else:
                step("V", lambda e, ub=ub, par=par, i=i: e.tensor_mul(
                    act_sb[:, i, :], sg_t[par][:, :], ub[:, :]), "c_am")
        for hc in range(4):
            m = dn_use[0]
            par = m % 2
            wait("T", "ddn%d" % par, dn_targets[m])
            dn_use[0] += 1
            if hc == 0:
                wait("T", "c_am", led["c_am"])
            for tc in range(8):
                dpar = tc % 2
                db = PA[:, 512 * dpar:512 * (dpar + 1)]
                wait("T", "c_ya", max(ya_start, led["c_ya"] - 1))

                def dmm(e, db=db, tc=tc, par=par):
                    last = None
                    for p in range(8):
                        last = e.matmul(db,
                                        act_sb[:, p, 128 * tc:128 * (tc + 1)],
                                        sdw[par][:, p, :],
                                        start=(p == 0), stop=(p == 7))
                    return last
                step("T", dmm, "c_dn")
                wait("V", "c_dn", led["c_dn"])
                step("V", lambda e, db=db, tc=tc, hc=hc: e.tensor_add(
                    Y_sb[:, tc, 512 * hc:512 * (hc + 1)],
                    Y_sb[:, tc, 512 * hc:512 * (hc + 1)], db), "c_ya")

    def p5(parts='rdwy'):
        wait("Y", "c_cc", 2)
        wait("Y", "c_ya", led["c_ya"])
        step("Y", lambda e: e.dma_start(
            out=AT_rb[:, :, :],
            in_=d_a2a_out.ap().rearrange("(f p) c -> p f c", p=128)), "drb", 16)
        step("Y", lambda e: e.dma_start(
            out=rl_sb[:, :, :].rearrange("r (k2 k1) (jj c) -> r k2 k1 jj c",
                                         k2=8, jj=2),
            in_=d_rl_out.ap().rearrange("(k2 k1 r jj) c -> r k2 k1 jj c",
                                        k2=8, k1=2, r=2, jj=2)), "drb", 16)
        if 'w' not in parts:
            return
        wob_targets = []
        for cc2 in range(4):
            par = cc2 % 2
            if cc2 < 2:
                wait("G", "c_ya", led["c_ya"])
            else:
                wait("G", "c_wo", 8 * (cc2 - 1))
            step("G", lambda e, cc2=cc2, par=par: e.dma_start(
                out=wob[par][:, :, :],
                in_=d_wout.ap()[cc2].rearrange("(f p) c -> p f c", p=128)),
                "dgu%d" % par, 16)
            wob_targets.append(led["dgu%d" % par])
        if 'd' not in parts:
            return
        wait("T", "drb", 32)
        wait("T", "c_ya", led["c_ya"])
        for k in range(16):
            def dbc(e, k=k):
                e.matmul(PB[:, 0:512], sel2[:, :], rl_sb[:, k, 0:512],
                         start=True, stop=True)
                return e.matmul(PB[:, 512:1024], sel2[:, :], rl_sb[:, k, 512:1024],
                                start=True, stop=True)
            step("T", dbc, "c_bc")
            wait("V", "c_bc", led["c_bc"])
            step("V", lambda e, k=k: e.tensor_mul(
                AT_rb[:, k, :], AT_rb[:, k, :], PB[:, :]), "c_dv")
            wait("T", "c_dv", led["c_dv"])
        wo_n = 0
        for cc2 in range(4):
            par = cc2 % 2
            wait("T", "dgu%d" % par, wob_targets[cc2])
            for tc in range(8):
                dpar = tc % 2
                db = PA[:, 512 * dpar:512 * (dpar + 1)]
                wait("T", "c_ya", led["c_ya"] - 1)

                def wmm(e, db=db, tc=tc, par=par):
                    last = None
                    for f in range(16):
                        last = e.matmul(db, AT_rb[:, f, 128 * tc:128 * (tc + 1)],
                                        wob[par][:, f, :],
                                        start=(f == 0), stop=(f == 15))
                    return last
                step("T", wmm, "c_wo")
                wo_n += 1
                wait("V", "c_wo", led["c_wo"])
                step("V", lambda e, db=db, tc=tc, cc2=cc2: e.tensor_add(
                    Y_sb[:, tc, 512 * cc2:512 * (cc2 + 1)],
                    Y_sb[:, tc, 512 * cc2:512 * (cc2 + 1)], db), "c_ya")
        if "RB" in tap_t:
            run("G", lambda e: e.wait_ge(sems["c_dv"], led["c_dv"]))
            step("G", lambda e: e.dma_start(
                out=tap_t["RB"].ap(),
                in_=AT_rb[:, :, :].rearrange("p f c -> p (f c)")), "dtap", 16)
        if 'y' not in parts:
            return
        wait("Y", "c_ya", led["c_ya"])
        for c in range(8):
            step("Y", lambda e, c=c: e.dma_start(
                out=d_y.ap()[128 * c:128 * (c + 1), :], in_=Y_sb[:, c, :]),
                "dyo", 16)
        if tap_t:
            wait("Y", "dtap", led["dtap"])
        wait("Y", "dyo", led["dyo"])

    p0()
    p1()
    if phases >= 2:
        barrier()
        p2()
        barrier()
        p25()
    if phases >= 3:
        p3_xn_own()
        p3_router()
    if phases >= 4:
        first = True
        if "sh" not in skip:
            for qq in range(4):
                ffn_weights("sh", qq, first=first)
                ffn_block("sh", qq)
                first = False
        if "ex" not in skip:
            for ex in range(8):
                ffn_weights("ex", ex, first=first)
                ffn_block("ex", ex)
                first = False
    if phases >= 5:
        p5(p5_parts)
    else:
        if tap_t:
            wait("Y", "dtap", led["dtap"])
        for k in "TVAYG":
            plan[k].append(lambda e, ss=sems["bar"]: e.nop().then_inc(ss, 1))
        wait("Y", "bar", led["bar"] + 5)
        led["bar"] += 5
        step("Y", lambda e: e.dma_start(out=d_y.ap()[0:128, :], in_=x_stage[0][:, :]),
             "dyo", 16)
        wait("Y", "dyo", led["dyo"])

    with nc.Block() as blk:
        @blk.tensor
        def _(e):
            for f in plan["T"]:
                f(e)

        @blk.vector
        def _(e):
            for f in plan["V"]:
                f(e)

        @blk.scalar
        def _(e):
            for f in plan["A"]:
                f(e)

        @blk.sync
        def _(e):
            for f in plan["Y"]:
                f(e)

        @blk.gpsimd
        def _(e):
            for f in plan["G"]:
                f(e)

    return nc


def _host_consts():
    bf = ml_dtypes.bfloat16
    inv_freq = (1.0 / 10000.0 ** (np.arange(0, 32, 2, dtype=np.float64) / 32.0))
    t = np.arange(S, dtype=np.float64)
    freqs = np.outer(t, inv_freq)
    cos = np.cos(freqs)
    sin = np.sin(freqs)
    cs1 = np.zeros((128, S), np.float32)
    cs2 = np.zeros((128, S), np.float32)
    for r in range(128):
        d = r % 64
        if d < 16:
            cs1[r] = cos[:, d]
            cs2[r] = -sin[:, d]
        elif d < 32:
            cs1[r] = cos[:, d - 16]
            cs2[r] = sin[:, d - 16]
        else:
            cs1[r] = 1.0
            cs2[r] = 0.0
    swap = np.zeros((128, 128), np.float32)
    for b in (0, 64):
        for i in range(16):
            swap[b + 16 + i, b + i] = 1.0
            swap[b + i, b + 16 + i] = 1.0
    masks = np.zeros((4, 128, 512), np.float32)
    for v in range(4):
        i = np.arange(128)[:, None]
        j = np.arange(512)[None, :]
        masks[v] = np.where(128 * v + i > j, MASK_NEG, 0.0)
    sel2 = np.zeros((2, 128), np.float32)
    sel2[0, 0:64] = 1.0
    sel2[1, 64:128] = 1.0
    return {
        "cs1": cs1.astype(bf), "cs2": cs2.astype(bf),
        "masks": masks.astype(bf),
        "identb": np.eye(128, dtype=bf),
        "ident32": np.eye(128, dtype=np.float32),
        "swapm": swap.astype(bf),
        "sel2": sel2.astype(bf),
        "ones1": np.ones((1, 128), np.float32),
    }


def _prep_inputs(inputs):
    f = np.float32
    x = np.ascontiguousarray(np.asarray(inputs["L0_attn_x"], f).reshape(S, D))
    Wqkv = np.asarray(inputs["L0_attn_Wqkv"], f)
    bqkv = np.asarray(inputs["L0_attn_bqkv"], f)
    Wout = np.asarray(inputs["L0_attn_Wout"], f)
    bout = np.asarray(inputs["L0_attn_bout"], f)
    wg = np.asarray(inputs["L0_ffn_w_gate"], f)
    wgu = np.asarray(inputs["L0_ffn_w_gate_up"], f)
    wdn = np.asarray(inputs["L0_ffn_w_down"], f)
    wsg = np.asarray(inputs["L0_ffn_w_sh_gate_up"], f)
    wsd = np.asarray(inputs["L0_ffn_w_sh_down"], f)

    consts = _host_consts()
    shared = dict(consts)
    shared["x"] = x
    shared["wgate"] = np.ascontiguousarray(wg)
    shared["bout"] = np.ascontiguousarray(bout.reshape(1, D))
    shared["wout"] = np.ascontiguousarray(
        Wout.reshape(D, 4, 512).transpose(1, 0, 2))
    g = wgu[:, :, 0:1024].reshape(8, 16, 128, 8, 128)
    u = wgu[:, :, 1024:2048].reshape(8, 16, 128, 8, 128)
    gu = np.concatenate([g, u], axis=4)
    shared["wgu"] = np.ascontiguousarray(gu.transpose(0, 3, 1, 2, 4))
    shared["wdn"] = np.ascontiguousarray(
        wdn.reshape(8, 8, 128, 4, 512).transpose(0, 3, 1, 2, 4))
    gs = wsg[:, 0:4096].reshape(16, 128, 32, 128)
    us = wsg[:, 4096:8192].reshape(16, 128, 32, 128)
    gus = np.concatenate([gs, us], axis=3)
    shared["wsg"] = np.ascontiguousarray(
        gus.transpose(2, 0, 1, 3).reshape(4, 8, 16, 128, 256))
    shared["wsd"] = np.ascontiguousarray(
        wsd.reshape(4, 8, 128, 4, 512).transpose(0, 3, 1, 2, 4))

    in_maps = []
    for c in range(NCORE):
        m = dict(shared)
        m["x_own"] = np.ascontiguousarray(x[TOK * c:TOK * (c + 1)])
        cols = []
        bc = []
        for sec in range(3):
            lo = 2048 * sec + 256 * c
            cols.append(Wqkv[:, lo:lo + 256])
            bc.append(bqkv[lo:lo + 256])
        m["wqkv"] = np.ascontiguousarray(np.concatenate(cols, axis=1))
        bqkv_t = np.concatenate(bc)
        m["bqkv"] = np.ascontiguousarray(
            bqkv_t.reshape(6, 128).T)
        in_maps.append(m)
    return in_maps


def kernel(**inputs):
    nc = build()
    in_maps = _prep_inputs(inputs)
    res = run_bass_kernel_spmd(nc, in_maps, core_ids=list(range(NCORE)))
    y = np.concatenate([res.results[c]["y"] for c in range(NCORE)], axis=0)
    return y.reshape(1, S, D).astype(np.float32)
